# version 02
# baseline (speedup 1.0000x reference)
import numpy as np
import concourse.bass as bass
import concourse.mybir as mybir
from concourse.bass_utils import run_bass_kernel_spmd
from contextlib import ExitStack

F32 = mybir.dt.float32
BF16 = mybir.dt.bfloat16
I32 = mybir.dt.int32
AF = mybir.ActivationFunctionType
OP = mybir.AluOpType

B, S, D = 4, 4096, 1024
H, DH = 16, 64
HID = 3072
TOK = 2048
NPAIR = 8
NQC = TOK // 512
NKT = S // 128
NQT = TOK // 128
NHC = HID // 128
EPS = np.float32(1e-5)

A_XT = 0
A_WP = [32768, 34816]
A_QK = [36864, 43008]
A_V4 = 49152
A_AOT = 65792
A_EXP = [82176, 83200]
A_TMP = 84224
A_VW = 85248
A_WO = 0
A_Z1 = 8192
A_XM = 24576
A_XMT = 40960
A_TRASH = 82176
A_HACT = 0
A_WD = 57344
A_WG = [83200, 84224]
A_WU = [85248, 86272]
A_SILU = [87296, 87808]
A_Z2 = 88320
ARENA = 96512

P_XRES = [0, 1024]
P_CLT = 0
P_BC = [2048, 3072]
P_OUTF = [2048, 3072]
P_ST = 4096
FPA = 4224


def _f_consts():
    n = DH
    f = np.arange(n // 2 + 1, dtype=np.float64)[:, None]
    a = np.arange(n, dtype=np.float64)[None, :]
    w = np.ones((n // 2 + 1, 1)); w[1:n // 2] = 2.0
    ang = 2 * np.pi * f * a / n
    clt = np.concatenate([w * np.cos(ang) / n, w * np.sin(ang) / n], 0)
    rt = np.concatenate([np.cos(ang), np.sin(ang)], 0)
    return clt.astype(np.float32), rt.astype(np.float32)


class Sem:
    def __init__(self, h):
        self.h = h
        self.n = 0


def build():
    nc = bass.Bass("TRN2", target_bir_lowering=False)
    d_xT = nc.dram_tensor("xT", [D, S], F32, kind="ExternalInput").ap()
    d_xres = nc.dram_tensor("xres", [TOK, D], F32, kind="ExternalInput").ap()
    d_wqkv = nc.dram_tensor("wqkv", [D, 3 * D], F32, kind="ExternalInput").ap()
    d_wout = nc.dram_tensor("wout", [D, D], F32, kind="ExternalInput").ap()
    d_bout = nc.dram_tensor("bout", [1, D], F32, kind="ExternalInput").ap()
    d_mlog = nc.dram_tensor("mlog", [33, 1], F32, kind="ExternalInput").ap()
    d_clt = nc.dram_tensor("clt", [66, 64], F32, kind="ExternalInput").ap()
    d_rt = nc.dram_tensor("rt_", [66, 64], F32, kind="ExternalInput").ap()
    d_wg = nc.dram_tensor("wgT", [D, HID], I32, kind="ExternalInput").ap()
    d_wu = nc.dram_tensor("wuT", [D, HID], I32, kind="ExternalInput").ap()
    d_wd = nc.dram_tensor("wdT", [HID, D], I32, kind="ExternalInput").ap()
    d_id = nc.dram_tensor("ident", [128, 128], F32, kind="ExternalInput").ap()
    d_out = nc.dram_tensor("out", [TOK, D], F32, kind="ExternalOutput").ap()
    d_scr = nc.dram_tensor("scr", [4, 1024], F32).ap()

    with ExitStack() as stk:
        def sbt(nm, shp, dt):
            return stk.enter_context(nc.sbuf_tensor(nm, shp, dt))

        ar = sbt("ar", [128, ARENA], BF16)
        fp = sbt("fp", [128, FPA], F32)
        id_b = sbt("id_b", [128, 128], BF16)
        F2 = sbt("F2", [128, 128], BF16)
        onesr = sbt("onesr", [1, 128], BF16)
        rt_b = sbt("rt_b", [66, 64], BF16)
        lhsF = sbt("lhsF", [66, 64], BF16)
        m2 = sbt("m2", [66, 1], F32)
        mrow = sbt("mrow", [66, 1], F32)
        T = [stk.enter_context(nc.psum_tensor(f"T{i}", [128, 1024], F32))
             for i in range(4)]

        names = ("g", "y", "sc", "pv", "qa", "f2", "pr", "tr", "gu", "dn",
                 "ev", "vv", "rc", "nm", "z1", "xm", "te", "ha", "z2", "of",
                 "ex", "sq", "si", "ti")
        SM = {n: Sem(stk.enter_context(nc.semaphore("s_" + n))) for n in names}
        (g, y, sc, pv, qa, f2s, pr, tr, gu, dn, ev, vv, rc, nm, z1, xm, te,
         ha, z2, of, ex, sq, si, ti) = (SM[n] for n in names)

        ops = []

        def E(eng, waits, body, incs=()):
            waits = [(s.h, v) for (s, v) in waits if v > 0]
            incs2 = []
            for s, k in incs:
                s.n += k
                incs2.append((s.h, k))

            def fn(e, waits=waits, incs2=incs2, body=body):
                for h, v in waits:
                    e.wait_ge(h, v)
                ins = body(e)
                for h, k in incs2:
                    ins.then_inc(h, k)
            ops.append((eng, fn))

        def gdma(dst, src, waits=()):
            E("gpsimd", waits, lambda e, d=dst, s=src: e.dma_start(d, s),
              [(g, 16)])
            return g.n

        def ydma(dst, src, waits=()):
            E("sync", waits, lambda e, d=dst, s=src: e.dma_start(d, s),
              [(y, 16)])
            return y.n

        def chain(eng, waits, body):
            E(eng, list(waits) + [(ti, ti.n)], body, [(ti, 1)])

        ydma(mrow[0:33, :], d_mlog)
        ydma(mrow[33:66, :], d_mlog)
        ydma(fp[0:66, P_CLT:P_CLT + 64], d_clt)
        y_consts = y.n
        gdma(rt_b[:, :], d_rt)
        gdma(id_b[:, :], d_id)
        g_consts = g.n
        for c in range(8):
            gdma(ar[:, A_XT + c * S:A_XT + (c + 1) * S],
                 d_xT[c * 128:(c + 1) * 128, :])
        g_xt = g.n

        chain("scalar", [(y, y_consts)],
              lambda e: e.activation(m2[:, :], mrow[:, :], AF.Sigmoid))
        chain("vector", [],
              lambda e: e.tensor_scalar(lhsF[:, :], fp[0:66, P_CLT:P_CLT + 64],
                                        m2[:, 0:1], None, OP.mult))
        chain("tensor", [(g, g_consts)],
              lambda e: e.matmul(T[0][0:64, 0:64], lhsF[:, :], rt_b[:, :],
                                 start=True, stop=True))
        chain("vector", [], lambda e: e.memset(F2[:, :], 0.0))
        chain("vector", [],
              lambda e: e.tensor_copy(F2[0:64, 0:64], T[0][0:64, 0:64]))
        chain("vector", [],
              lambda e: e.tensor_copy(F2[64:128, 64:128], T[0][0:64, 0:64]))
        chain("vector", [], lambda e: e.memset(onesr[:, :], 1.0))
        f2_ready = ti.n

        qa_after_pair = []
        ev_after_pair = []
        vv_after_batch = {}
        scrw_y = []
        bc_y = []

        def v_batch(vb):
            vin = d_wqkv[:, 2 * D + vb * 512:2 * D + (vb + 1) * 512].rearrange(
                "(c p) m -> p c m", p=128)
            vout = ar[:, A_VW:A_VW + 4096].rearrange("p (c m) -> p c m", c=8)
            w = [(pv, 4 * NQC * NKT)] if vb == 1 else []
            gv = gdma(vout, vin, list(w))
            ones_ap = ar[:, A_V4:A_V4 + 32 * 520].rearrange(
                "p (m d) -> p m d", d=65)[:, :, 64:65]
            E("vector", list(w),
              lambda e, o=ones_ap: e.memset(o, 1.0), [(vv, 1)])
            evac_hist = []
            for tt in range(32):
                par = tt % 2
                acc = T[par][:, 0:512]
                for c in range(8):
                    w2 = []
                    if c == 0:
                        w2 = [(g, gv)]
                        if tt >= 2:
                            w2.append((vv, evac_hist[tt - 2]))
                        if vb == 1:
                            w2.append((pv, 4 * NQC * NKT))
                        else:
                            w2.append((ti, f2_ready))
                    lh = ar[:, A_XT + c * S + tt * 128:
                            A_XT + c * S + tt * 128 + 128]
                    rh = ar[:, A_VW + c * 512:A_VW + (c + 1) * 512]
                    E("tensor", w2,
                      lambda e, o=acc, l=lh, r=rh, st=(c == 0), sp=(c == 7):
                      e.matmul(o, l, r, start=st, stop=sp),
                      [(qa, 1)] if c == 7 else ())
                dst = ar[:, A_V4 + tt * 520:A_V4 + (tt + 1) * 520].rearrange(
                    "p (m d) -> p m d", d=65)[:, :, 0:64]
                src = acc.rearrange("p (m d) -> p m d", d=64)
                E("vector", [(qa, qa.n)],
                  lambda e, o=dst, i2=src: e.tensor_copy(o, i2), [(vv, 1)])
                evac_hist.append(vv.n)
            vv_after_batch[vb] = vv.n

        for hp in range(NPAIR):
            wb = A_WP[hp % 2]
            qkb = A_QK[hp % 2]
            w = [(qa, qa_after_pair[hp - 2])] if hp >= 2 else []
            qin = d_wqkv[:, hp * 128:(hp + 1) * 128].rearrange(
                "(c p) m -> p c m", p=128)
            kin = d_wqkv[:, D + hp * 128:D + (hp + 1) * 128].rearrange(
                "(c p) m -> p c m", p=128)
            qo = ar[:, wb:wb + 2048].rearrange("p (c t m) -> p c t m",
                                               c=8, t=2)[:, :, 0, :]
            ko = ar[:, wb:wb + 2048].rearrange("p (c t m) -> p c t m",
                                               c=8, t=2)[:, :, 1, :]
            gdma(qo, qin, w)
            gw = gdma(ko, kin)
            if hp % 4 == 0:
                v_batch(hp // 4)
            chunks = [(1, n) for n in range(8)] + [(0, n) for n in range(4)]
            ev_hist = []
            for i, (kind, n) in enumerate(chunks):
                par = i % 2
                acc = T[par][:, 0:512]
                fo = T[par][:, 512:1024]
                for c in range(8):
                    w2 = []
                    if c == 0:
                        w2 = [(g, gw), (g, g_xt)]
                        if i >= 2:
                            w2.append((ev, ev_hist[i - 2]))
                        else:
                            w2.append((vv, vv_after_batch[hp // 4]))
                            w2.append((ex, ex.n))
                    lh = ar[:, wb + c * 256 + kind * 128:
                            wb + c * 256 + kind * 128 + 128]
                    rh = ar[:, A_XT + c * S + n * 512:
                            A_XT + c * S + n * 512 + 512]
                    E("tensor", w2,
                      lambda e, o=acc, l=lh, r=rh, st=(c == 0), sp=(c == 7):
                      e.matmul(o, l, r, start=st, stop=sp),
                      [(qa, 1)] if c == 7 else ())
                E("vector", [(qa, qa.n)],
                  lambda e, o=ar[:, A_TMP:A_TMP + 512], i2=acc:
                  e.tensor_copy(o, i2), [(ev, 1)])
                E("tensor", [(ev, ev.n)],
                  lambda e, o=fo, r=ar[:, A_TMP:A_TMP + 512]:
                  e.matmul(o, F2[:, :], r, start=True, stop=True), [(f2s, 1)])
                dcol = qkb + (2048 + n * 512 if kind == 1 else n * 512)
                E("vector", [(f2s, f2s.n)],
                  lambda e, o=ar[:, dcol:dcol + 512], i2=fo:
                  e.tensor_copy(o, i2), [(ev, 1)])
                ev_hist.append(ev.n)
            qa_after_pair.append(qa.n)
            ev_after_pair.append(ev.n)

            vvn = vv_after_batch[hp // 4]
            sA = (hp % 4) * 130
            sB = sA + 65
            for qc in range(NQC):
                inst = hp * NQC + qc
                ob = T[2 + qc % 2]
                for kt in range(NKT):
                    t = inst * NKT + kt
                    tb = T[t % 2]
                    w2 = [(ex, t - 1)] if t >= 2 else []
                    if kt == 0 and qc == 0:
                        w2.append((ev, ev_after_pair[hp]))
                        w2.append((vv, vvn))
                    kcol = qkb + 2048 + kt * 128
                    qcol = qkb + qc * 512
                    E("tensor", w2,
                      lambda e, o=tb[:, 0:512], l=ar[0:64, kcol:kcol + 128],
                      r=ar[0:64, qcol:qcol + 512]:
                      e.matmul(o, l, r, start=True, stop=True,
                               tile_position=(0, 0)))
                    E("tensor", [],
                      lambda e, o=tb[:, 512:1024],
                      l=ar[64:128, kcol:kcol + 128],
                      r=ar[64:128, qcol:qcol + 512]:
                      e.matmul(o, l, r, start=True, stop=True,
                               tile_position=(64, 0)), [(sc, 1)])
                    eb = A_EXP[t % 2]
                    E("scalar", [(sc, sc.n)],
                      lambda e, o=ar[:, eb:eb + 1024], i2=tb[:, 0:1024]:
                      e.activation(o, i2, AF.Exp, scale=np.float32(0.125)),
                      [(ex, 1)])
                    w3 = [(ex, ex.n)]
                    if kt == 0 and inst >= 2:
                        w3.append((nm, inst - 1))
                    E("tensor", w3,
                      lambda e, o=ob[0:65, 0:512],
                      l=ar[:, A_V4 + kt * 520 + sA:A_V4 + kt * 520 + sA + 65],
                      r=ar[:, eb:eb + 512], st=(kt == 0), sp=(kt == NKT - 1):
                      e.matmul(o, l, r, start=st, stop=sp))
                    E("tensor", [],
                      lambda e, o=ob[0:65, 512:1024],
                      l=ar[:, A_V4 + kt * 520 + sB:A_V4 + kt * 520 + sB + 65],
                      r=ar[:, eb + 512:eb + 1024], st=(kt == 0),
                      sp=(kt == NKT - 1):
                      e.matmul(o, l, r, start=st, stop=sp), [(pv, 1)])
                bcb = P_BC[inst % 2]
                w4 = [(pv, pv.n)]
                if inst >= 2:
                    w4.append((y, scrw_y[inst - 2]))
                E("vector", w4,
                  lambda e, o=fp[64:65, bcb:bcb + 512], i2=ob[64:65, 0:512]:
                  e.reciprocal(o, i2))
                E("vector", [],
                  lambda e, o=fp[64:65, bcb + 512:bcb + 1024],
                  i2=ob[64:65, 512:1024]:
                  e.reciprocal(o, i2), [(rc, 1)])
                scr_row = d_scr[inst % 4:inst % 4 + 1, :]
                w5 = [(rc, rc.n)]
                if inst >= 4:
                    w5.append((y, bc_y[inst - 4]))
                scrw_y.append(ydma(scr_row, fp[64:65, bcb:bcb + 1024], w5))
                w6 = [(y, y.n)]
                if inst >= 2:
                    w6.append((nm, inst - 1))
                ydma(fp[0:64, bcb:bcb + 512],
                     scr_row[:, 0:512].to_broadcast((64, 512)), w6)
                bc_y.append(ydma(fp[0:64, bcb + 512:bcb + 1024],
                                 scr_row[:, 512:1024].to_broadcast((64, 512))))
                aot = A_AOT + hp * 2048 + qc * 512
                E("vector", [(y, bc_y[inst])],
                  lambda e, o=ar[0:64, aot:aot + 512], i2=ob[0:64, 0:512],
                  b2=fp[0:64, bcb:bcb + 512]:
                  e.tensor_tensor(o, i2, b2, OP.mult))
                E("vector", [],
                  lambda e, o=ar[64:128, aot:aot + 512],
                  i2=ob[0:64, 512:1024], b2=fp[0:64, bcb + 512:bcb + 1024]:
                  e.tensor_tensor(o, i2, b2, OP.mult), [(nm, 1)])

        win = d_wout.rearrange("(c p) m -> p c m", p=128)
        wo_out = ar[:, A_WO:A_WO + 8192].rearrange("p (c m) -> p c m", c=8)
        gdma(wo_out, win, [(qa, qa.n)])
        BOUT = A_Z1 + 15 * 1024
        gwo = gdma(ar[0:1, BOUT:BOUT + 1024], d_bout)

        z1_hist = []
        for qt in range(NQT):
            par = qt % 2
            tp = T[par]
            w2 = [(z1, z1_hist[qt - 2])] if qt >= 2 else []
            xres_y = ydma(fp[:, P_XRES[par]:P_XRES[par] + 1024],
                          d_xres[qt * 128:(qt + 1) * 128, :], w2)
            for c in range(8):
                w3 = []
                if c == 0:
                    w3 = [(g, gwo), (nm, 32)]
                    if qt >= 2:
                        w3.append((z1, z1_hist[qt - 2]))
                lh = ar[:, A_AOT + c * 2048 + qt * 128:
                        A_AOT + c * 2048 + qt * 128 + 128]
                E("tensor", w3,
                  lambda e, o=tp[:, 0:512], l=lh,
                  r=ar[:, A_WO + c * 1024:A_WO + c * 1024 + 512], st=(c == 0):
                  e.matmul(o, l, r, start=st, stop=False))
                E("tensor", [],
                  lambda e, o=tp[:, 512:1024], l=lh,
                  r=ar[:, A_WO + c * 1024 + 512:A_WO + (c + 1) * 1024],
                  st=(c == 0):
                  e.matmul(o, l, r, start=st, stop=False))
            E("tensor", [],
              lambda e, o=tp[:, 0:512]:
              e.matmul(o, onesr[:, :], ar[0:1, BOUT:BOUT + 512], start=False,
                       stop=True))
            E("tensor", [],
              lambda e, o=tp[:, 512:1024]:
              e.matmul(o, onesr[:, :], ar[0:1, BOUT + 512:BOUT + 1024],
                       start=False, stop=True), [(pr, 1)])
            E("vector", [(pr, pr.n), (y, xres_y)],
              lambda e, o=ar[:, A_Z1 + qt * 1024:A_Z1 + (qt + 1) * 1024],
              i2=tp[:, 0:1024], x2=fp[:, P_XRES[par]:P_XRES[par] + 1024],
              a2=fp[:, P_ST + qt:P_ST + qt + 1]:
              e.scalar_tensor_tensor(o, i2, 1.0, x2, OP.mult, OP.add,
                                     accum_out=a2), [(z1, 1)])
            z1_hist.append(z1.n)
            E("scalar", [(z1, z1.n)],
              lambda e, o=ar[:, A_TRASH:A_TRASH + 1024],
              i2=ar[:, A_Z1 + qt * 1024:A_Z1 + (qt + 1) * 1024],
              a2=fp[:, P_ST + 16 + qt:P_ST + 16 + qt + 1]:
              e.activation(o, i2, AF.Square, accum_out=a2), [(sq, 1)])

        def ln_stats(nt):
            def cS(k):
                return fp[:, P_ST + k * 16:P_ST + k * 16 + nt]
            chain("vector", [(z1, z1.n), (sq, sq.n), (z2, z2.n), (of, of.n)],
                  lambda e: e.tensor_scalar(cS(2), cS(0), 1.0 / D,
                                            None, OP.mult))
            chain("vector", [],
                  lambda e: e.tensor_tensor(cS(3), cS(2), cS(2), OP.mult))
            chain("vector", [],
                  lambda e: e.scalar_tensor_tensor(cS(4), cS(1), 1.0 / D,
                                                   cS(3), OP.mult,
                                                   OP.subtract))
            chain("scalar", [],
                  lambda e: e.activation(cS(5), cS(4), AF.Ln, bias=EPS))
            chain("scalar", [],
                  lambda e: e.activation(cS(6), cS(5), AF.Exp,
                                         scale=np.float32(-0.5)))
            chain("vector", [],
                  lambda e: e.scalar_tensor_tensor(cS(7), cS(2), -1.0,
                                                   cS(6), OP.mult, OP.mult))
            return ti.n

        st1 = ln_stats(16)
        te_hist = []
        for qt in range(NQT):
            E("vector", [(ti, st1)],
              lambda e, o=ar[:, A_XM + qt * 1024:A_XM + (qt + 1) * 1024],
              i2=ar[:, A_Z1 + qt * 1024:A_Z1 + (qt + 1) * 1024],
              s1=fp[:, P_ST + 96 + qt:P_ST + 96 + qt + 1],
              s2=fp[:, P_ST + 112 + qt:P_ST + 112 + qt + 1]:
              e.tensor_scalar(o, i2, s1, s2, OP.mult, OP.add), [(xm, 1)])
            tbit = T[2 + qt % 2][:, 0:512].bitcast(BF16)
            for j in range(8):
                w2 = []
                if j == 0:
                    w2 = [(xm, xm.n)]
                    if qt >= 2:
                        w2.append((te, te_hist[qt - 2]))
                    else:
                        w2.append((nm, 32))
                E("tensor", w2,
                  lambda e, o=tbit[:, j * 128:(j + 1) * 128],
                  i2=ar[:, A_XM + qt * 1024 + j * 128:
                         A_XM + qt * 1024 + j * 128 + 128]:
                  e.transpose(o, i2, id_b[:, :]),
                  [(tr, 1)] if j == 7 else ())
            xmt = ar[:, A_XMT:A_XMT + 16384].rearrange(
                "p (c t) -> p c t", c=8)[:, :, qt * 128:(qt + 1) * 128]
            E("vector", [(tr, tr.n)],
              lambda e, o=xmt,
              i2=tbit[:, 0:1024].rearrange("p (c t) -> p c t", c=8):
              e.tensor_copy(o, i2), [(te, 1)])
            te_hist.append(te.n)

        gwd = 0
        for part in range(4):
            wdin = d_wd.rearrange("(c p) m -> p c m",
                                  p=128)[:, part * 6:(part + 1) * 6, :]
            wdo = ar[:, A_WD:A_WD + 24576].rearrange(
                "p (c m) -> p c m", c=24)[:, part * 6:(part + 1) * 6, :]
            gwd = gdma(wdo, wdin, [(pr, 16)] if part == 0 else ())

        ha_hist = {}
        gu_hist = {}
        out_y_hist = []
        for th in range(2):
            for hc in range(NHC):
                wgb = A_WG[hc % 2]
                wub = A_WU[hc % 2]
                key_w = (th, hc - 2) if hc >= 2 else (th - 1, NHC - 2 + hc)
                w2 = [(gu, gu_hist[key_w])] if key_w in gu_hist else []
                gin = d_wg[:, hc * 128:(hc + 1) * 128].rearrange(
                    "(c p) m -> p c m", p=128)
                uin = d_wu[:, hc * 128:(hc + 1) * 128].rearrange(
                    "(c p) m -> p c m", p=128)
                gdma(ar[:, wgb:wgb + 1024].rearrange("p (c m) -> p c m", c=8),
                     gin, w2)
                gld = gdma(
                    ar[:, wub:wub + 1024].rearrange("p (c m) -> p c m", c=8),
                    uin)
                tg = T[2 * (hc % 2)]
                tu = T[2 * (hc % 2) + 1]
                for n in range(2):
                    for c in range(8):
                        w3 = []
                        if n == 0 and c == 0:
                            w3 = [(g, gld), (te, te.n)]
                            key_b = (th, hc - 2) if hc >= 2 else (
                                (th - 1, NHC - 2 + hc) if th > 0 else None)
                            if key_b is not None and key_b in ha_hist:
                                w3.append((ha, ha_hist[key_b]))
                            if th == 1 and hc < 2:
                                w3.append((z2, 8))
                        xcol = A_XMT + th * 1024 + n * 512
                        E("tensor", w3,
                          lambda e, o=tg[:, n * 512:(n + 1) * 512],
                          l=ar[:, wgb + c * 128:wgb + (c + 1) * 128],
                          r=ar[:, xcol + c * 2048:xcol + c * 2048 + 512],
                          st=(c == 0), sp=(c == 7):
                          e.matmul(o, l, r, start=st, stop=sp))
                        E("tensor", [],
                          lambda e, o=tu[:, n * 512:(n + 1) * 512],
                          l=ar[:, wub + c * 128:wub + (c + 1) * 128],
                          r=ar[:, xcol + c * 2048:xcol + c * 2048 + 512],
                          st=(c == 0), sp=(c == 7):
                          e.matmul(o, l, r, start=st, stop=sp),
                          [(gu, 1)] if (c == 7 and n == 1) else ())
                gu_hist[(th, hc)] = gu.n
                for n in range(2):
                    sb2 = A_SILU[n]
                    w3 = [(gu, gu.n)]
                    key_s = (th, hc - 1) if hc >= 1 else (th - 1, NHC - 1)
                    if key_s in ha_hist:
                        w3.append((ha, ha_hist[key_s]))
                    E("scalar", w3,
                      lambda e, o=ar[:, sb2:sb2 + 512],
                      i2=tg[:, n * 512:(n + 1) * 512]:
                      e.activation(o, i2, AF.Silu,
                                   scale=np.float32(1.0 / 127.0)), [(si, 1)])
                    hcol = A_HACT + hc * 1024 + n * 512
                    E("vector", [(si, si.n)],
                      lambda e, o=ar[:, hcol:hcol + 512],
                      i2=ar[:, sb2:sb2 + 512],
                      u2=tu[:, n * 512:(n + 1) * 512]:
                      e.tensor_tensor(o, i2, u2, OP.mult), [(ha, 1)])
                ha_hist[(th, hc)] = ha.n
            z2_hist = []
            for tt in range(8):
                par = tt % 2
                tp = T[par]
                for hc in range(NHC):
                    w3 = []
                    if hc == 0:
                        w3 = [(ha, ha_hist[(th, NHC - 1)]), (g, gwd)]
                        if tt >= 2:
                            w3.append((z2, z2_hist[tt - 2]))
                        else:
                            w3.append((gu, gu_hist[(th, NHC - 1)]))
                    lh = ar[:, A_HACT + hc * 1024 + tt * 128:
                            A_HACT + hc * 1024 + tt * 128 + 128]
                    E("tensor", w3,
                      lambda e, o=tp[:, 0:512], l=lh,
                      r=ar[:, A_WD + hc * 1024:A_WD + hc * 1024 + 512],
                      st=(hc == 0), sp=(hc == NHC - 1):
                      e.matmul(o, l, r, start=st, stop=sp))
                    E("tensor", [],
                      lambda e, o=tp[:, 512:1024], l=lh,
                      r=ar[:, A_WD + hc * 1024 + 512:A_WD + (hc + 1) * 1024],
                      st=(hc == 0), sp=(hc == NHC - 1):
                      e.matmul(o, l, r, start=st, stop=sp),
                      [(dn, 1)] if hc == NHC - 1 else ())
                qt = th * 8 + tt
                E("vector", [(dn, dn.n)],
                  lambda e, o=ar[:, A_Z2 + tt * 1024:A_Z2 + (tt + 1) * 1024],
                  i2=tp[:, 0:1024],
                  x2=ar[:, A_XM + qt * 1024:A_XM + (qt + 1) * 1024],
                  a2=fp[:, P_ST + tt:P_ST + tt + 1]:
                  e.scalar_tensor_tensor(o, i2, 1.0 / 16129.0, x2,
                                         OP.mult, OP.add, accum_out=a2),
                  [(z2, 1)])
                z2_hist.append(z2.n)
                E("scalar", [(z2, z2.n)],
                  lambda e, o=ar[:, A_TRASH:A_TRASH + 1024],
                  i2=ar[:, A_Z2 + tt * 1024:A_Z2 + (tt + 1) * 1024],
                  a2=fp[:, P_ST + 16 + tt:P_ST + 16 + tt + 1]:
                  e.activation(o, i2, AF.Square, accum_out=a2), [(sq, 1)])
            st2 = ln_stats(8)
            for tt in range(8):
                par = tt % 2
                qt = th * 8 + tt
                w3 = [(ti, st2)]
                if qt >= 2:
                    w3.append((y, out_y_hist[qt - 2]))
                E("vector", w3,
                  lambda e, o=fp[:, P_OUTF[par]:P_OUTF[par] + 1024],
                  i2=ar[:, A_Z2 + tt * 1024:A_Z2 + (tt + 1) * 1024],
                  s1=fp[:, P_ST + 96 + tt:P_ST + 96 + tt + 1],
                  s2=fp[:, P_ST + 112 + tt:P_ST + 112 + tt + 1]:
                  e.tensor_scalar(o, i2, s1, s2, OP.mult, OP.add), [(of, 1)])
                out_y_hist.append(
                    ydma(d_out[qt * 128:(qt + 1) * 128, :],
                         fp[:, P_OUTF[par]:P_OUTF[par] + 1024], [(of, of.n)]))

        E("sync", [(y, y.n)], lambda e: None)
        E("gpsimd", [(y, y.n)], lambda e: None)

        block = stk.enter_context(nc.Block())
        for name in ("tensor", "vector", "scalar", "sync", "gpsimd"):
            def runner(eng, name=name):
                for e2, fn in ops:
                    if e2 == name:
                        fn(eng)
            getattr(block, name)(runner)
    return nc


_NC_CACHE = None


def kernel(L0_attn_x, L0_attn_w_qkv, L0_attn_w_out, L0_attn_b_out,
           L0_attn_mask_logits, L0_ffn_w_gate, L0_ffn_w_up, L0_ffn_w_down):
    x = np.ascontiguousarray(np.asarray(L0_attn_x, dtype=np.float32))
    wqkv = np.ascontiguousarray(np.asarray(L0_attn_w_qkv, dtype=np.float32))
    wout = np.ascontiguousarray(np.asarray(L0_attn_w_out, dtype=np.float32))
    bout = np.ascontiguousarray(
        np.asarray(L0_attn_b_out, dtype=np.float32).reshape(1, D))
    mlog = np.ascontiguousarray(
        np.asarray(L0_attn_mask_logits, dtype=np.float32).reshape(33, 1))
    wgT = np.ascontiguousarray(np.asarray(L0_ffn_w_gate, dtype=np.int32).T)
    wuT = np.ascontiguousarray(np.asarray(L0_ffn_w_up, dtype=np.int32).T)
    wdT = np.ascontiguousarray(np.asarray(L0_ffn_w_down, dtype=np.int32).T)
    clt, rt = _f_consts()
    ident = np.eye(128, dtype=np.float32)

    global _NC_CACHE
    if _NC_CACHE is None:
        _NC_CACHE = build()
    nc = _NC_CACHE

    in_maps = []
    for c in range(8):
        b, hf = c // 2, c % 2
        xb = x[b]
        if hf == 0:
            xTc = np.ascontiguousarray(xb.T)
        else:
            xTc = np.ascontiguousarray(
                np.concatenate([xb[TOK:], xb[:TOK]], 0).T)
        in_maps.append({
            "xT": xTc,
            "xres": np.ascontiguousarray(xb[hf * TOK:(hf + 1) * TOK]),
            "wqkv": wqkv, "wout": wout, "bout": bout, "mlog": mlog,
            "clt": clt, "rt_": rt, "wgT": wgT, "wuT": wuT, "wdT": wdT,
            "ident": ident})
    res = run_bass_kernel_spmd(nc, in_maps, core_ids=list(range(8)))
    out = np.empty((B, S, D), np.float32)
    for c in range(8):
        b, hf = c // 2, c % 2
        out[b, hf * TOK:(hf + 1) * TOK] = np.asarray(res.results[c]["out"])
    return out


# version 08
# speedup vs baseline: 2.2375x; 2.2375x over previous
import numpy as np
import concourse.bass as bass
import concourse.mybir as mybir
from concourse.bass_utils import run_bass_kernel_spmd
from contextlib import ExitStack

F32 = mybir.dt.float32
BF16 = mybir.dt.bfloat16
I32 = mybir.dt.int32
AF = mybir.ActivationFunctionType
OP = mybir.AluOpType

B, S, D = 4, 4096, 1024
H, DH = 16, 64
HID = 3072
TOK = 2048
NPAIR = 8
NQC = TOK // 512
NKT = S // 128
NQT = TOK // 128
NHC = HID // 128
EPS = np.float32(1e-5)

A_XT = 0
A_WP = [32768, 34816]
A_QK = [36864, 43008]
A_V4 = 49152
A_AOT = 65792
A_EXP = [82176, 83200]
A_TMP = 84224
A_VW = 85248
A_WO = 0
A_Z1 = 8192
A_XM = 24576
A_XMT = 40960
A_TRASH = 82176
A_HACT = 0
A_WD = 57344
A_WG = [83200, 84224]
A_WU = [85248, 86272]
A_SILU = [87296, 87808]
A_Z2 = 88320
ARENA = 96512

P_XRES = [0, 1024]
P_CLT = 0
P_BC = [2048, 3072]
P_OUTF = [2048, 3072]
P_ST = 4096
P_DEN = 4224
FPA = 4352


def _f_consts():
    n = DH
    f = np.arange(n // 2 + 1, dtype=np.float64)[:, None]
    a = np.arange(n, dtype=np.float64)[None, :]
    w = np.ones((n // 2 + 1, 1)); w[1:n // 2] = 2.0
    ang = 2 * np.pi * f * a / n
    clt = np.concatenate([w * np.cos(ang) / n, w * np.sin(ang) / n], 0)
    rt = np.concatenate([np.cos(ang), np.sin(ang)], 0)
    return clt.astype(np.float32), rt.astype(np.float32)


class Sem:
    def __init__(self, h):
        self.h = h
        self.n = 0


def build():
    nc = bass.Bass("TRN2", target_bir_lowering=False)
    d_xT = nc.dram_tensor("xT", [D, S], F32, kind="ExternalInput").ap()
    d_xres = nc.dram_tensor("xres", [TOK, D], F32, kind="ExternalInput").ap()
    d_wqkv = nc.dram_tensor("wqkv", [D, 3 * D], F32, kind="ExternalInput").ap()
    d_wout = nc.dram_tensor("wout", [D, D], F32, kind="ExternalInput").ap()
    d_bout = nc.dram_tensor("bout", [1, D], F32, kind="ExternalInput").ap()
    d_mlog = nc.dram_tensor("mlog", [33, 1], F32, kind="ExternalInput").ap()
    d_clt = nc.dram_tensor("clt", [66, 64], F32, kind="ExternalInput").ap()
    d_rt = nc.dram_tensor("rt_", [66, 64], F32, kind="ExternalInput").ap()
    d_wg = nc.dram_tensor("wgT", [D, HID], I32, kind="ExternalInput").ap()
    d_wu = nc.dram_tensor("wuT", [D, HID], I32, kind="ExternalInput").ap()
    d_wd = nc.dram_tensor("wdT", [HID, D], I32, kind="ExternalInput").ap()
    d_id = nc.dram_tensor("ident", [128, 128], F32, kind="ExternalInput").ap()
    d_out = nc.dram_tensor("out", [TOK, D], F32, kind="ExternalOutput").ap()
    d_scr = nc.dram_tensor("scr", [8, 1024], F32).ap()

    with ExitStack() as stk:
        def sbt(nm, shp, dt):
            return stk.enter_context(nc.sbuf_tensor(nm, shp, dt))

        ar = sbt("ar", [128, ARENA], BF16)
        fp = sbt("fp", [128, FPA], F32)
        id_b = sbt("id_b", [128, 128], BF16)
        F2 = sbt("F2", [128, 128], BF16)
        onesr = sbt("onesr", [1, 128], BF16)
        rt_b = sbt("rt_b", [66, 64], BF16)
        lhsF = sbt("lhsF", [66, 64], BF16)
        m2 = sbt("m2", [66, 1], F32)
        mrow = sbt("mrow", [66, 1], F32)
        T = [stk.enter_context(nc.psum_tensor(f"T{i}", [128, 1024], F32))
             for i in range(4)]

        names = ("g", "y", "sc", "pv", "qa", "f2", "pr", "tr", "gu", "dn",
                 "ev", "vv", "rc", "nm", "z1", "xm", "te", "ha", "z2", "of",
                 "ex", "sq", "si", "ti", "oe")
        SM = {n: Sem(stk.enter_context(nc.semaphore("s_" + n))) for n in names}
        (g, y, sc, pv, qa, f2s, pr, tr, gu, dn, ev, vv, rc, nm, z1, xm, te,
         ha, z2, of, ex, sq, si, ti, oe) = (SM[n] for n in names)

        ops = []

        def E(eng, waits, body, incs=()):
            waits = [(s.h, v) for (s, v) in waits if v > 0]
            incs2 = []
            for s, k in incs:
                s.n += k
                incs2.append((s.h, k))

            def fn(e, waits=waits, incs2=incs2, body=body):
                for h, v in waits:
                    e.wait_ge(h, v)
                ins = body(e)
                for h, k in incs2:
                    ins.then_inc(h, k)
            ops.append((eng, fn))

        def gdma(dst, src, waits=()):
            E("gpsimd", waits, lambda e, d=dst, s=src: e.dma_start(d, s),
              [(g, 16)])
            return g.n

        def ydma(dst, src, waits=()):
            E("sync", waits, lambda e, d=dst, s=src: e.dma_start(d, s),
              [(y, 16)])
            return y.n

        def chain(eng, waits, body):
            E(eng, list(waits) + [(ti, ti.n)], body, [(ti, 1)])

        ydma(mrow[0:33, :], d_mlog)
        ydma(mrow[33:66, :], d_mlog)
        ydma(fp[0:66, P_CLT:P_CLT + 64], d_clt)
        y_consts = y.n
        gdma(rt_b[:, :], d_rt)
        gdma(id_b[:, :], d_id)
        g_consts = g.n
        for c in range(8):
            gdma(ar[:, A_XT + c * S:A_XT + (c + 1) * S],
                 d_xT[c * 128:(c + 1) * 128, :])
        g_xt = g.n

        chain("scalar", [(y, y_consts)],
              lambda e: e.activation(m2[:, :], mrow[:, :], AF.Sigmoid))
        chain("vector", [],
              lambda e: e.tensor_scalar(lhsF[:, :], fp[0:66, P_CLT:P_CLT + 64],
                                        m2[:, 0:1], None, OP.mult))
        chain("tensor", [(g, g_consts)],
              lambda e: e.matmul(T[0][0:64, 0:64], lhsF[:, :], rt_b[:, :],
                                 start=True, stop=True))
        chain("vector", [], lambda e: e.memset(F2[:, :], 0.0))
        chain("vector", [],
              lambda e: e.tensor_copy(F2[0:64, 0:64], T[0][0:64, 0:64]))
        chain("vector", [],
              lambda e: e.tensor_copy(F2[64:128, 64:128], T[0][0:64, 0:64]))
        chain("vector", [], lambda e: e.memset(onesr[:, :], 1.0))
        f2_ready = ti.n

        A_STG = 89344
        qa_after_pair = {}
        ev_after_pair = {}
        vv_after_batch = {}
        exp_hist = {}
        pv_hist = {}
        scrw_y = []
        den_y = []
        wb_y = []
        rc_hist = []
        bc_y = []

        def v_batch(vb):
            vin = d_wqkv[:, 2 * D + vb * 512:2 * D + (vb + 1) * 512].rearrange(
                "(c p) m -> p c m", p=128)
            vout = ar[:, A_VW:A_VW + 4096].rearrange("p (c m) -> p c m", c=8)
            w = [(pv, 4 * NQC * NKT)] if vb == 1 else []
            gv = gdma(vout, vin, list(w))
            ones_ap = ar[:, A_V4:A_V4 + 32 * 520].rearrange(
                "p (m d) -> p m d", d=65)[:, :, 64:65]
            E("vector", list(w),
              lambda e, o=ones_ap: e.memset(o, 1.0), [(vv, 1)])
            evac_hist = []
            for tt in range(32):
                acc = T[tt % 2][:, 0:512]
                for c in range(8):
                    w2 = []
                    if c == 0:
                        w2 = [(g, gv)]
                        if tt >= 2:
                            w2.append((vv, evac_hist[tt - 2]))
                        if vb == 1:
                            w2.append((pv, 4 * NQC * NKT))
                        else:
                            w2.append((ti, f2_ready))
                    lh = ar[:, A_XT + c * S + tt * 128:
                            A_XT + c * S + tt * 128 + 128]
                    rh = ar[:, A_VW + c * 512:A_VW + (c + 1) * 512]
                    E("tensor", w2,
                      lambda e, o=acc, l=lh, r=rh, st=(c == 0), sp=(c == 7):
                      e.matmul(o, l, r, start=st, stop=sp),
                      [(qa, 1)] if c == 7 else ())
                dst = ar[:, A_V4 + tt * 520:A_V4 + (tt + 1) * 520].rearrange(
                    "p (m d) -> p m d", d=65)[:, :, 0:64]
                src2 = acc.rearrange("p (m d) -> p m d", d=64)
                E("vector", [(qa, qa.n)],
                  lambda e, o=dst, i2=src2: e.tensor_copy(o, i2), [(vv, 1)])
                evac_hist.append(vv.n)
            vv_after_batch[vb] = vv.n

        def wload(hp):
            wb = A_WP[hp % 2]
            w = [(qa, qa_after_pair[hp - 2])] if hp >= 2 else []
            qin = d_wqkv[:, hp * 128:(hp + 1) * 128].rearrange(
                "(c p) m -> p c m", p=128)
            kin = d_wqkv[:, D + hp * 128:D + (hp + 1) * 128].rearrange(
                "(c p) m -> p c m", p=128)
            qo = ar[:, wb:wb + 2048].rearrange("p (c t m) -> p c t m",
                                               c=8, t=2)[:, :, 0, :]
            ko = ar[:, wb:wb + 2048].rearrange("p (c t m) -> p c t m",
                                               c=8, t=2)[:, :, 1, :]
            gdma(qo, qin, w)
            return gdma(ko, kin)

        CHUNKS = [(1, n) for n in range(8)] + [(0, n) for n in range(4)]

        def prologue_qkv(gw):
            hp, wb, qkb = 0, A_WP[0], A_QK[0]
            ev_hist = []
            fparts = []

            def emit_f2(j):
                kind, n = CHUNKS[j]
                fo = T[j % 2][:, 512:1024]
                E("tensor", [(ev, ev_hist[j])],
                  lambda e, o=fo, r=ar[:, A_TMP:A_TMP + 512]:
                  e.matmul(o, F2[:, :], r, start=True, stop=True), [(f2s, 1)])
                dcol = qkb + (2048 + n * 512 if kind == 1 else n * 512)
                E("vector", [(f2s, f2s.n)],
                  lambda e, o=ar[:, dcol:dcol + 512], i2=fo:
                  e.tensor_copy(o, i2), [(ev, 1)])

            evB_hist = []
            for i, (kind, n) in enumerate(CHUNKS):
                acc = T[i % 2][:, 0:512]
                for c in range(8):
                    w2 = []
                    if c == 0:
                        w2 = [(g, gw), (g, g_xt)]
                        if i >= 2:
                            w2.append((ev, ev_hist[i - 2]))
                        else:
                            w2.append((vv, vv_after_batch[0]))
                    lh = ar[:, wb + c * 256 + kind * 128:
                            wb + c * 256 + kind * 128 + 128]
                    rh = ar[:, A_XT + c * S + n * 512:
                            A_XT + c * S + n * 512 + 512]
                    E("tensor", w2,
                      lambda e, o=acc, l=lh, r=rh, st=(c == 0), sp=(c == 7):
                      e.matmul(o, l, r, start=st, stop=sp),
                      [(qa, 1)] if c == 7 else ())
                wA = [(qa, qa.n)]
                if i >= 1:
                    wA.append((f2s, f2s.n))
                E("vector", wA,
                  lambda e, o=ar[:, A_TMP:A_TMP + 512], i2=acc:
                  e.tensor_copy(o, i2), [(ev, 1)])
                ev_hist.append(ev.n)
                if i >= 1:
                    emit_f2(i - 1)
            emit_f2(len(CHUNKS) - 1)
            qa_after_pair[0] = qa.n
            ev_after_pair[0] = ev.n

        t3 = {"b0": None, "b1": None, "tmp": None}

        def make_micro_steps(hp, gw):
            wb = A_WP[hp % 2]
            qkb = A_QK[hp % 2]
            evA = {}
            steps = []
            for i, (kind, n) in enumerate(CHUNKS):
                for c in range(8):
                    def mm_step(i=i, kind=kind, n=n, c=c):
                        acc = T[3][:, 0:512]
                        w2 = []
                        if c == 0:
                            w2 = [(g, gw)]
                            if t3["b0"] is not None:
                                w2.append(t3["b0"])
                        lh = ar[:, wb + c * 256 + kind * 128:
                                wb + c * 256 + kind * 128 + 128]
                        rh = ar[:, A_XT + c * S + n * 512:
                                A_XT + c * S + n * 512 + 512]
                        E("tensor", w2,
                          lambda e, o=acc, l=lh, r=rh, st=(c == 0),
                          sp=(c == 7): e.matmul(o, l, r, start=st, stop=sp),
                          [(qa, 1)] if c == 7 else ())
                        if c == 7:
                            wA = [(qa, qa.n)]
                            if t3["tmp"] is not None:
                                wA.append(t3["tmp"])
                            E("vector", wA,
                              lambda e, o=ar[:, A_TMP:A_TMP + 512], i2=acc:
                              e.tensor_copy(o, i2), [(ev, 1)])
                            evA[i] = ev.n
                            t3["b0"] = (ev, ev.n)
                    steps.append(mm_step)

                def f2_step(i=i, kind=kind, n=n):
                    fo = T[3][:, 512:1024]
                    w2 = [(ev, evA[i])]
                    if t3["b1"] is not None:
                        w2.append(t3["b1"])
                    E("tensor", w2,
                      lambda e, o=fo, r=ar[:, A_TMP:A_TMP + 512]:
                      e.matmul(o, F2[:, :], r, start=True, stop=True),
                      [(f2s, 1)])
                    t3["tmp"] = (f2s, f2s.n)
                    dcol = qkb + (2048 + n * 512 if kind == 1 else n * 512)
                    E("vector", [(f2s, f2s.n)],
                      lambda e, o=ar[:, dcol:dcol + 512], i2=fo:
                      e.tensor_copy(o, i2), [(ev, 1)])
                    t3["b1"] = (ev, ev.n)

                steps.append(f2_step)
            return steps

        def emit_scores(hp, t):
            qc, kt = divmod(t, 32)
            Tg = hp * 128 + t
            tb = T[Tg % 2]
            qkb = A_QK[hp % 2]
            w2 = []
            if Tg >= 2:
                w2.append((ex, exp_hist[Tg - 2]))
            if t == 0:
                w2.append((ev, ev_after_pair[hp]))
                w2.append((vv, vv_after_batch[hp // 4]))
            kcol = qkb + 2048 + kt * 128
            qcol = qkb + qc * 512
            E("tensor", w2,
              lambda e, o=tb[:, 0:512], l=ar[0:64, kcol:kcol + 128],
              r=ar[0:64, qcol:qcol + 512]:
              e.matmul(o, l, r, start=True, stop=True,
                       tile_position=(0, 0)))
            E("tensor", [],
              lambda e, o=tb[:, 512:1024],
              l=ar[64:128, kcol:kcol + 128],
              r=ar[64:128, qcol:qcol + 512]:
              e.matmul(o, l, r, start=True, stop=True,
                       tile_position=(64, 0)), [(sc, 1)])
            eb = A_EXP[Tg % 2]
            we = [(sc, sc.n)]
            if Tg >= 2:
                we.append((pv, pv_hist[Tg - 2]))
            E("scalar", we,
              lambda e, o=ar[:, eb:eb + 1024], i2=tb[:, 0:1024]:
              e.activation(o, i2, AF.Exp, scale=np.float32(0.125)),
              [(ex, 1)])
            exp_hist[Tg] = ex.n

        def emit_pv(hp, t):
            qc, kt = divmod(t, 32)
            Tg = hp * 128 + t
            inst = hp * NQC + qc
            ob = T[2]
            eb = A_EXP[Tg % 2]
            sA = (hp % 4) * 130
            sB = sA + 65
            w3 = [(ex, exp_hist[Tg])]
            if kt == 0 and inst >= 1:
                w3.append((oe, inst))
            if t == 0:
                w3.append((vv, vv_after_batch[hp // 4]))
            E("tensor", w3,
              lambda e, o=ob[0:65, 0:512],
              l=ar[:, A_V4 + kt * 520 + sA:A_V4 + kt * 520 + sA + 65],
              r=ar[:, eb:eb + 512], st=(kt == 0), sp=(kt == NKT - 1):
              e.matmul(o, l, r, start=st, stop=sp))
            E("tensor", [],
              lambda e, o=ob[0:65, 512:1024],
              l=ar[:, A_V4 + kt * 520 + sB:A_V4 + kt * 520 + sB + 65],
              r=ar[:, eb + 512:eb + 1024], st=(kt == 0),
              sp=(kt == NKT - 1):
              e.matmul(o, l, r, start=st, stop=sp), [(pv, 1)])
            pv_hist[Tg] = pv.n

        def emit_stage_normalize(hp, qc):
            inst = hp * NQC + qc
            ob = T[2]
            w4 = [(pv, pv.n)]
            if inst >= 1:
                w4.append((nm, inst))
            E("vector", w4,
              lambda e, o=ar[0:65, A_STG:A_STG + 1024],
              i2=ob[0:65, 0:1024]: e.tensor_copy(o, i2), [(oe, 1)])
            oe_here = oe.n
            bcb = P_BC[inst % 2]

            def d_recipA(inst=inst, bcb=bcb, oe_here=oe_here):
                w5 = [(oe, oe_here)]
                if inst >= 2:
                    w5.append((y, scrw_y[inst - 2]))
                E("vector", w5,
                  lambda e, o=fp[64:65, bcb:bcb + 512],
                  i2=ar[64:65, A_STG:A_STG + 512]: e.reciprocal(o, i2))

            def d_recipB(inst=inst, bcb=bcb):
                E("vector", [],
                  lambda e, o=fp[64:65, bcb + 512:bcb + 1024],
                  i2=ar[64:65, A_STG + 512:A_STG + 1024]:
                  e.reciprocal(o, i2), [(rc, 1)])
                scr_row = d_scr[inst % 4:inst % 4 + 1, :]
                w6 = [(rc, rc.n)]
                if inst >= 4:
                    w6.append((y, bc_y[inst - 4]))
                scrw_y.append(ydma(scr_row, fp[64:65, bcb:bcb + 1024], w6))
                w7 = [(y, y.n)]
                if inst >= 2:
                    w7.append((nm, inst - 1))
                ydma(fp[0:64, bcb:bcb + 512],
                     scr_row[:, 0:512].to_broadcast((64, 512)), w7)
                bc_y.append(ydma(fp[0:64, bcb + 512:bcb + 1024],
                                 scr_row[:, 512:1024].to_broadcast((64, 512))))

            def d_mults(inst=inst, bcb=bcb, hp=hp, qc=qc):
                aot = A_AOT + hp * 2048 + qc * 512
                E("vector", [(y, bc_y[inst])],
                  lambda e, o=ar[0:64, aot:aot + 512],
                  i2=ar[0:64, A_STG:A_STG + 512],
                  b2=fp[0:64, bcb:bcb + 512]:
                  e.tensor_tensor(o, i2, b2, OP.mult))
                E("vector", [],
                  lambda e, o=ar[64:128, aot:aot + 512],
                  i2=ar[0:64, A_STG + 512:A_STG + 1024],
                  b2=fp[0:64, bcb + 512:bcb + 1024]:
                  e.tensor_tensor(o, i2, b2, OP.mult), [(nm, 1)])

            return [(1, d_recipA), (3, d_recipB), (7, d_mults)]

        gw0 = wload(0)
        v_batch(0)
        prologue_qkv(gw0)
        for hp in range(NPAIR):
            pend = []
            if hp < NPAIR - 1:
                gw = wload(hp + 1)
                pend = make_micro_steps(hp + 1, gw)
            emit_scores(hp, 0)
            dnorm = []
            for t in range(128):
                if t + 1 < 128:
                    emit_scores(hp, t + 1)
                if pend:
                    pend.pop(0)()
                while dnorm and dnorm[0][0] <= t:
                    dnorm.pop(0)[1]()
                emit_pv(hp, t)
                if t % 32 == 31:
                    for off, fn2 in emit_stage_normalize(hp, t // 32):
                        dnorm.append((t + off, fn2))
            while pend:
                pend.pop(0)()
            while dnorm:
                dnorm.pop(0)[1]()
            if hp < NPAIR - 1:
                qa_after_pair[hp + 1] = qa.n
                ev_after_pair[hp + 1] = ev.n
            if hp == 3:
                v_batch(1)

        win = d_wout.rearrange("(c p) m -> p c m", p=128)
        wo_out = ar[:, A_WO:A_WO + 8192].rearrange("p (c m) -> p c m", c=8)
        gdma(wo_out, win, [(qa, qa.n)])
        BOUT = A_Z1 + 15 * 1024
        gwo = gdma(ar[0:1, BOUT:BOUT + 1024], d_bout)

        z1_hist = []
        for qt in range(NQT):
            par = qt % 2
            tp = T[par]
            w2 = [(z1, z1_hist[qt - 2])] if qt >= 2 else []
            xres_y = ydma(fp[:, P_XRES[par]:P_XRES[par] + 1024],
                          d_xres[qt * 128:(qt + 1) * 128, :], w2)
            for c in range(8):
                w3 = []
                if c == 0:
                    w3 = [(g, gwo), (nm, 32)]
                    if qt >= 2:
                        w3.append((z1, z1_hist[qt - 2]))
                lh = ar[:, A_AOT + c * 2048 + qt * 128:
                        A_AOT + c * 2048 + qt * 128 + 128]
                E("tensor", w3,
                  lambda e, o=tp[:, 0:512], l=lh,
                  r=ar[:, A_WO + c * 1024:A_WO + c * 1024 + 512], st=(c == 0):
                  e.matmul(o, l, r, start=st, stop=False))
                E("tensor", [],
                  lambda e, o=tp[:, 512:1024], l=lh,
                  r=ar[:, A_WO + c * 1024 + 512:A_WO + (c + 1) * 1024],
                  st=(c == 0):
                  e.matmul(o, l, r, start=st, stop=False))
            E("tensor", [],
              lambda e, o=tp[:, 0:512]:
              e.matmul(o, onesr[:, :], ar[0:1, BOUT:BOUT + 512], start=False,
                       stop=True))
            E("tensor", [],
              lambda e, o=tp[:, 512:1024]:
              e.matmul(o, onesr[:, :], ar[0:1, BOUT + 512:BOUT + 1024],
                       start=False, stop=True), [(pr, 1)])
            E("vector", [(pr, pr.n), (y, xres_y)],
              lambda e, o=ar[:, A_Z1 + qt * 1024:A_Z1 + (qt + 1) * 1024],
              i2=tp[:, 0:1024], x2=fp[:, P_XRES[par]:P_XRES[par] + 1024],
              a2=fp[:, P_ST + qt:P_ST + qt + 1]:
              e.scalar_tensor_tensor(o, i2, 1.0, x2, OP.mult, OP.add,
                                     accum_out=a2), [(z1, 1)])
            z1_hist.append(z1.n)
            E("scalar", [(z1, z1.n)],
              lambda e, o=ar[:, A_TRASH:A_TRASH + 1024],
              i2=ar[:, A_Z1 + qt * 1024:A_Z1 + (qt + 1) * 1024],
              a2=fp[:, P_ST + 16 + qt:P_ST + 16 + qt + 1]:
              e.activation(o, i2, AF.Square, accum_out=a2), [(sq, 1)])

        def ln_stats(nt):
            def cS(k):
                return fp[:, P_ST + k * 16:P_ST + k * 16 + nt]
            chain("vector", [(z1, z1.n), (sq, sq.n), (z2, z2.n), (of, of.n)],
                  lambda e: e.tensor_scalar(cS(2), cS(0), 1.0 / D,
                                            None, OP.mult))
            chain("vector", [],
                  lambda e: e.tensor_tensor(cS(3), cS(2), cS(2), OP.mult))
            chain("vector", [],
                  lambda e: e.scalar_tensor_tensor(cS(4), cS(1), 1.0 / D,
                                                   cS(3), OP.mult,
                                                   OP.subtract))
            chain("scalar", [],
                  lambda e: e.activation(cS(5), cS(4), AF.Ln, bias=EPS))
            chain("scalar", [],
                  lambda e: e.activation(cS(6), cS(5), AF.Exp,
                                         scale=np.float32(-0.5)))
            chain("vector", [],
                  lambda e: e.scalar_tensor_tensor(cS(7), cS(2), -1.0,
                                                   cS(6), OP.mult, OP.mult))
            return ti.n

        st1 = ln_stats(16)
        te_hist = []
        for qt in range(NQT):
            E("vector", [(ti, st1)],
              lambda e, o=ar[:, A_XM + qt * 1024:A_XM + (qt + 1) * 1024],
              i2=ar[:, A_Z1 + qt * 1024:A_Z1 + (qt + 1) * 1024],
              s1=fp[:, P_ST + 96 + qt:P_ST + 96 + qt + 1],
              s2=fp[:, P_ST + 112 + qt:P_ST + 112 + qt + 1]:
              e.tensor_scalar(o, i2, s1, s2, OP.mult, OP.add), [(xm, 1)])
            tbit = T[2 + qt % 2][:, 0:512].bitcast(BF16)
            for j in range(8):
                w2 = []
                if j == 0:
                    w2 = [(xm, xm.n)]
                    if qt >= 2:
                        w2.append((te, te_hist[qt - 2]))
                    else:
                        w2.append((nm, 32))
                E("tensor", w2,
                  lambda e, o=tbit[:, j * 128:(j + 1) * 128],
                  i2=ar[:, A_XM + qt * 1024 + j * 128:
                         A_XM + qt * 1024 + j * 128 + 128]:
                  e.transpose(o, i2, id_b[:, :]),
                  [(tr, 1)] if j == 7 else ())
            xmt = ar[:, A_XMT:A_XMT + 16384].rearrange(
                "p (c t) -> p c t", c=8)[:, :, qt * 128:(qt + 1) * 128]
            E("vector", [(tr, tr.n)],
              lambda e, o=xmt,
              i2=tbit[:, 0:1024].rearrange("p (c t) -> p c t", c=8):
              e.tensor_copy(o, i2), [(te, 1)])
            te_hist.append(te.n)

        gwd = 0
        for part in range(4):
            wdin = d_wd.rearrange("(c p) m -> p c m",
                                  p=128)[:, part * 6:(part + 1) * 6, :]
            wdo = ar[:, A_WD:A_WD + 24576].rearrange(
                "p (c m) -> p c m", c=24)[:, part * 6:(part + 1) * 6, :]
            gwd = gdma(wdo, wdin, [(pr, 16)] if part == 0 else ())

        ha_hist = {}
        gu_hist = {}
        out_y_hist = []
        for th in range(2):
            for hc in range(NHC):
                wgb = A_WG[hc % 2]
                wub = A_WU[hc % 2]
                key_w = (th, hc - 2) if hc >= 2 else (th - 1, NHC - 2 + hc)
                w2 = [(gu, gu_hist[key_w])] if key_w in gu_hist else []
                gin = d_wg[:, hc * 128:(hc + 1) * 128].rearrange(
                    "(c p) m -> p c m", p=128)
                uin = d_wu[:, hc * 128:(hc + 1) * 128].rearrange(
                    "(c p) m -> p c m", p=128)
                gdma(ar[:, wgb:wgb + 1024].rearrange("p (c m) -> p c m", c=8),
                     gin, w2)
                gld = gdma(
                    ar[:, wub:wub + 1024].rearrange("p (c m) -> p c m", c=8),
                    uin)
                tg = T[2 * (hc % 2)]
                tu = T[2 * (hc % 2) + 1]
                for n in range(2):
                    for c in range(8):
                        w3 = []
                        if n == 0 and c == 0:
                            w3 = [(g, gld), (te, te.n)]
                            key_b = (th, hc - 2) if hc >= 2 else (
                                (th - 1, NHC - 2 + hc) if th > 0 else None)
                            if key_b is not None and key_b in ha_hist:
                                w3.append((ha, ha_hist[key_b]))
                            if th == 1 and hc < 2:
                                w3.append((z2, 8))
                        xcol = A_XMT + th * 1024 + n * 512
                        E("tensor", w3,
                          lambda e, o=tg[:, n * 512:(n + 1) * 512],
                          l=ar[:, wgb + c * 128:wgb + (c + 1) * 128],
                          r=ar[:, xcol + c * 2048:xcol + c * 2048 + 512],
                          st=(c == 0), sp=(c == 7):
                          e.matmul(o, l, r, start=st, stop=sp))
                        E("tensor", [],
                          lambda e, o=tu[:, n * 512:(n + 1) * 512],
                          l=ar[:, wub + c * 128:wub + (c + 1) * 128],
                          r=ar[:, xcol + c * 2048:xcol + c * 2048 + 512],
                          st=(c == 0), sp=(c == 7):
                          e.matmul(o, l, r, start=st, stop=sp),
                          [(gu, 1)] if (c == 7 and n == 1) else ())
                gu_hist[(th, hc)] = gu.n
                for n in range(2):
                    sb2 = A_SILU[n]
                    w3 = [(gu, gu.n)]
                    key_s = (th, hc - 1) if hc >= 1 else (th - 1, NHC - 1)
                    if key_s in ha_hist:
                        w3.append((ha, ha_hist[key_s]))
                    E("scalar", w3,
                      lambda e, o=ar[:, sb2:sb2 + 512],
                      i2=tg[:, n * 512:(n + 1) * 512]:
                      e.activation(o, i2, AF.Silu,
                                   scale=np.float32(1.0 / 127.0)), [(si, 1)])
                    hcol = A_HACT + hc * 1024 + n * 512
                    E("vector", [(si, si.n)],
                      lambda e, o=ar[:, hcol:hcol + 512],
                      i2=ar[:, sb2:sb2 + 512],
                      u2=tu[:, n * 512:(n + 1) * 512]:
                      e.tensor_tensor(o, i2, u2, OP.mult), [(ha, 1)])
                ha_hist[(th, hc)] = ha.n
            z2_hist = []
            for tt in range(8):
                par = tt % 2
                tp = T[par]
                for hc in range(NHC):
                    w3 = []
                    if hc == 0:
                        w3 = [(ha, ha_hist[(th, NHC - 1)]), (g, gwd)]
                        if tt >= 2:
                            w3.append((z2, z2_hist[tt - 2]))
                        else:
                            w3.append((gu, gu_hist[(th, NHC - 1)]))
                    lh = ar[:, A_HACT + hc * 1024 + tt * 128:
                            A_HACT + hc * 1024 + tt * 128 + 128]
                    E("tensor", w3,
                      lambda e, o=tp[:, 0:512], l=lh,
                      r=ar[:, A_WD + hc * 1024:A_WD + hc * 1024 + 512],
                      st=(hc == 0), sp=(hc == NHC - 1):
                      e.matmul(o, l, r, start=st, stop=sp))
                    E("tensor", [],
                      lambda e, o=tp[:, 512:1024], l=lh,
                      r=ar[:, A_WD + hc * 1024 + 512:A_WD + (hc + 1) * 1024],
                      st=(hc == 0), sp=(hc == NHC - 1):
                      e.matmul(o, l, r, start=st, stop=sp),
                      [(dn, 1)] if hc == NHC - 1 else ())
                qt = th * 8 + tt
                E("vector", [(dn, dn.n)],
                  lambda e, o=ar[:, A_Z2 + tt * 1024:A_Z2 + (tt + 1) * 1024],
                  i2=tp[:, 0:1024],
                  x2=ar[:, A_XM + qt * 1024:A_XM + (qt + 1) * 1024],
                  a2=fp[:, P_ST + tt:P_ST + tt + 1]:
                  e.scalar_tensor_tensor(o, i2, 1.0 / 16129.0, x2,
                                         OP.mult, OP.add, accum_out=a2),
                  [(z2, 1)])
                z2_hist.append(z2.n)
                E("scalar", [(z2, z2.n)],
                  lambda e, o=ar[:, A_TRASH:A_TRASH + 1024],
                  i2=ar[:, A_Z2 + tt * 1024:A_Z2 + (tt + 1) * 1024],
                  a2=fp[:, P_ST + 16 + tt:P_ST + 16 + tt + 1]:
                  e.activation(o, i2, AF.Square, accum_out=a2), [(sq, 1)])
            st2 = ln_stats(8)
            for tt in range(8):
                par = tt % 2
                qt = th * 8 + tt
                w3 = [(ti, st2)]
                if qt >= 2:
                    w3.append((y, out_y_hist[qt - 2]))
                E("vector", w3,
                  lambda e, o=fp[:, P_OUTF[par]:P_OUTF[par] + 1024],
                  i2=ar[:, A_Z2 + tt * 1024:A_Z2 + (tt + 1) * 1024],
                  s1=fp[:, P_ST + 96 + tt:P_ST + 96 + tt + 1],
                  s2=fp[:, P_ST + 112 + tt:P_ST + 112 + tt + 1]:
                  e.tensor_scalar(o, i2, s1, s2, OP.mult, OP.add), [(of, 1)])
                out_y_hist.append(
                    ydma(d_out[qt * 128:(qt + 1) * 128, :],
                         fp[:, P_OUTF[par]:P_OUTF[par] + 1024], [(of, of.n)]))

        E("sync", [(y, y.n)], lambda e: None)
        E("gpsimd", [(y, y.n)], lambda e: None)

        block = stk.enter_context(nc.Block())
        for name in ("tensor", "vector", "scalar", "sync", "gpsimd"):
            def runner(eng, name=name):
                for e2, fn in ops:
                    if e2 == name:
                        fn(eng)
            getattr(block, name)(runner)
    return nc


_NC_CACHE = None


def kernel(L0_attn_x, L0_attn_w_qkv, L0_attn_w_out, L0_attn_b_out,
           L0_attn_mask_logits, L0_ffn_w_gate, L0_ffn_w_up, L0_ffn_w_down):
    x = np.ascontiguousarray(np.asarray(L0_attn_x, dtype=np.float32))
    wqkv = np.ascontiguousarray(np.asarray(L0_attn_w_qkv, dtype=np.float32))
    wout = np.ascontiguousarray(np.asarray(L0_attn_w_out, dtype=np.float32))
    bout = np.ascontiguousarray(
        np.asarray(L0_attn_b_out, dtype=np.float32).reshape(1, D))
    mlog = np.ascontiguousarray(
        np.asarray(L0_attn_mask_logits, dtype=np.float32).reshape(33, 1))
    wgT = np.ascontiguousarray(np.asarray(L0_ffn_w_gate, dtype=np.int32).T)
    wuT = np.ascontiguousarray(np.asarray(L0_ffn_w_up, dtype=np.int32).T)
    wdT = np.ascontiguousarray(np.asarray(L0_ffn_w_down, dtype=np.int32).T)
    clt, rt = _f_consts()
    ident = np.eye(128, dtype=np.float32)

    global _NC_CACHE
    if _NC_CACHE is None:
        _NC_CACHE = build()
    nc = _NC_CACHE

    in_maps = []
    for c in range(8):
        b, hf = c // 2, c % 2
        xb = x[b]
        if hf == 0:
            xTc = np.ascontiguousarray(xb.T)
        else:
            xTc = np.ascontiguousarray(
                np.concatenate([xb[TOK:], xb[:TOK]], 0).T)
        in_maps.append({
            "xT": xTc,
            "xres": np.ascontiguousarray(xb[hf * TOK:(hf + 1) * TOK]),
            "wqkv": wqkv, "wout": wout, "bout": bout, "mlog": mlog,
            "clt": clt, "rt_": rt, "wgT": wgT, "wuT": wuT, "wdT": wdT,
            "ident": ident})
    res = run_bass_kernel_spmd(nc, in_maps, core_ids=list(range(8)))
    out = np.empty((B, S, D), np.float32)
    for c in range(8):
        b, hf = c // 2, c % 2
        out[b, hf * TOK:(hf + 1) * TOK] = np.asarray(res.results[c]["out"])
    return out


# version 12
# speedup vs baseline: 2.2887x; 1.0229x over previous
import numpy as np
import concourse.bass as bass
import concourse.mybir as mybir
from concourse.bass_utils import run_bass_kernel_spmd
from contextlib import ExitStack

F32 = mybir.dt.float32
BF16 = mybir.dt.bfloat16
I32 = mybir.dt.int32
AF = mybir.ActivationFunctionType
OP = mybir.AluOpType

B, S, D = 4, 4096, 1024
H, DH = 16, 64
HID = 3072
TOK = 2048
NPAIR = 8
NQC = TOK // 512
NKT = S // 128
NQT = TOK // 128
NHC = HID // 128
EPS = np.float32(1e-5)

A_XT = 0
A_WP = [32768, 34816]
A_QK = [36864, 43008]
A_WO2 = 36864
A_WO3 = 85248
BOUT2 = 87296
A_V4 = 49152
A_AOT = 65792
A_EXP = [82176, 83200]
A_TMP = 84224
A_VW = 85248
A_WO = 0
A_Z1 = 8192
A_XM = 24576
A_XMT = 40960
A_TRASH = 82176
A_HACT = 0
A_WD = 57344
A_WG = [83200, 84224]
A_WU = [85248, 86272]
A_SILU = [87296, 87808]
A_Z2 = 88320
ARENA = 96512

P_XRES = [0, 1024]
P_CLT = 0
P_BC = [2048, 3072]
P_OUTF = [2048, 3072]
P_ST = 4096
P_DEN = 4224
FPA = 4352


def _f_consts():
    n = DH
    f = np.arange(n // 2 + 1, dtype=np.float64)[:, None]
    a = np.arange(n, dtype=np.float64)[None, :]
    w = np.ones((n // 2 + 1, 1)); w[1:n // 2] = 2.0
    ang = 2 * np.pi * f * a / n
    clt = np.concatenate([w * np.cos(ang) / n, w * np.sin(ang) / n], 0)
    rt = np.concatenate([np.cos(ang), np.sin(ang)], 0)
    return clt.astype(np.float32), rt.astype(np.float32)


class Sem:
    def __init__(self, h):
        self.h = h
        self.n = 0


def build():
    nc = bass.Bass("TRN2", target_bir_lowering=False)
    d_xT = nc.dram_tensor("xT", [D, S], F32, kind="ExternalInput").ap()
    d_xres = nc.dram_tensor("xres", [TOK, D], F32, kind="ExternalInput").ap()
    d_wqkv = nc.dram_tensor("wqkv", [D, 3 * D], F32, kind="ExternalInput").ap()
    d_wout = nc.dram_tensor("wout", [D, D], F32, kind="ExternalInput").ap()
    d_bout = nc.dram_tensor("bout", [1, D], F32, kind="ExternalInput").ap()
    d_mlog = nc.dram_tensor("mlog", [33, 1], F32, kind="ExternalInput").ap()
    d_clt = nc.dram_tensor("clt", [66, 64], F32, kind="ExternalInput").ap()
    d_rt = nc.dram_tensor("rt_", [66, 64], F32, kind="ExternalInput").ap()
    d_wg = nc.dram_tensor("wgT", [D, HID], I32, kind="ExternalInput").ap()
    d_wu = nc.dram_tensor("wuT", [D, HID], I32, kind="ExternalInput").ap()
    d_wd = nc.dram_tensor("wdT", [HID, D], I32, kind="ExternalInput").ap()
    d_id = nc.dram_tensor("ident", [128, 128], F32, kind="ExternalInput").ap()
    d_out = nc.dram_tensor("out", [TOK, D], F32, kind="ExternalOutput").ap()
    d_scr = nc.dram_tensor("scr", [8, 1024], F32).ap()

    with ExitStack() as stk:
        def sbt(nm, shp, dt):
            return stk.enter_context(nc.sbuf_tensor(nm, shp, dt))

        ar = sbt("ar", [128, ARENA], BF16)
        fp = sbt("fp", [128, FPA], F32)
        id_b = sbt("id_b", [128, 128], BF16)
        F2 = sbt("F2", [128, 128], BF16)
        onesr = sbt("onesr", [1, 128], BF16)
        rt_b = sbt("rt_b", [66, 64], BF16)
        lhsF = sbt("lhsF", [66, 64], BF16)
        m2 = sbt("m2", [66, 1], F32)
        mrow = sbt("mrow", [66, 1], F32)
        T = [stk.enter_context(nc.psum_tensor(f"T{i}", [128, 1024], F32))
             for i in range(4)]

        names = ("g", "y", "sc", "pv", "qa", "f2", "pr", "tr", "gu", "dn",
                 "ev", "vv", "rc", "nm", "z1", "xm", "te", "ha", "z2", "of",
                 "ex", "sq", "si", "ti", "oe")
        SM = {n: Sem(stk.enter_context(nc.semaphore("s_" + n))) for n in names}
        (g, y, sc, pv, qa, f2s, pr, tr, gu, dn, ev, vv, rc, nm, z1, xm, te,
         ha, z2, of, ex, sq, si, ti, oe) = (SM[n] for n in names)

        ops = []

        def E(eng, waits, body, incs=()):
            waits = [(s.h, v) for (s, v) in waits if v > 0]
            incs2 = []
            for s, k in incs:
                s.n += k
                incs2.append((s.h, k))

            def fn(e, waits=waits, incs2=incs2, body=body):
                for h, v in waits:
                    e.wait_ge(h, v)
                ins = body(e)
                for h, k in incs2:
                    ins.then_inc(h, k)
            ops.append((eng, fn))

        def gdma(dst, src, waits=()):
            E("gpsimd", waits, lambda e, d=dst, s=src: e.dma_start(d, s),
              [(g, 16)])
            return g.n

        def ydma(dst, src, waits=()):
            E("sync", waits, lambda e, d=dst, s=src: e.dma_start(d, s),
              [(y, 16)])
            return y.n

        def chain(eng, waits, body):
            E(eng, list(waits) + [(ti, ti.n)], body, [(ti, 1)])

        ydma(mrow[0:33, :], d_mlog)
        ydma(mrow[33:66, :], d_mlog)
        ydma(fp[0:66, P_CLT:P_CLT + 64], d_clt)
        y_consts = y.n
        gdma(rt_b[:, :], d_rt)
        gdma(id_b[:, :], d_id)
        g_consts = g.n

        chain("scalar", [(y, y_consts)],
              lambda e: e.activation(m2[:, :], mrow[:, :], AF.Sigmoid))
        chain("vector", [],
              lambda e: e.tensor_scalar(lhsF[:, :], fp[0:66, P_CLT:P_CLT + 64],
                                        m2[:, 0:1], None, OP.mult))
        chain("tensor", [(g, g_consts)],
              lambda e: e.matmul(T[0][0:64, 0:64], lhsF[:, :], rt_b[:, :],
                                 start=True, stop=True))
        chain("vector", [], lambda e: e.memset(F2[:, :], 0.0))
        chain("vector", [],
              lambda e: e.tensor_copy(F2[0:64, 0:64], T[0][0:64, 0:64]))
        chain("vector", [],
              lambda e: e.tensor_copy(F2[64:128, 64:128], T[0][0:64, 0:64]))
        chain("vector", [], lambda e: e.memset(onesr[:, :], 1.0))
        f2_ready = ti.n

        A_STG = 89344
        qa_after_pair = {}
        ev_after_pair = {}
        vv_after_batch = {}
        g_xt_q = []
        exp_hist = {}
        pv_hist = {}
        scrw_y = []
        den_y = []
        wb_y = []
        rc_hist = []
        bc_y = []

        def v_batch(vb):
            if vb == 1:
                vin = d_wqkv[:, 2 * D + 512:2 * D + 1024].rearrange(
                    "(c p) m -> p c m", p=128)
                vout = ar[:, A_VW:A_VW + 4096].rearrange(
                    "p (c m) -> p c m", c=8)
                w = [(pv, 4 * NQC * NKT)]
                gv = gdma(vout, vin, list(w))
            else:
                w = []
                gv = gv0
            ones_ap = ar[:, A_V4:A_V4 + 32 * 520].rearrange(
                "p (m d) -> p m d", d=65)[:, :, 64:65]
            E("vector", list(w),
              lambda e, o=ones_ap: e.memset(o, 1.0), [(vv, 1)])
            evac_hist = []
            for tt in range(32):
                acc = T[tt % 2][:, 0:512]
                for c in range(8):
                    w2 = []
                    if c == 0:
                        w2 = [(g, gv)]
                        if tt >= 2:
                            w2.append((vv, evac_hist[tt - 2]))
                        if vb == 1:
                            w2.append((pv, 4 * NQC * NKT))
                        else:
                            w2.append((ti, f2_ready))
                            w2.append((g, g_xt_q[tt // 8]))
                    lh = ar[:, A_XT + c * S + tt * 128:
                            A_XT + c * S + tt * 128 + 128]
                    rh = ar[:, A_VW + c * 512:A_VW + (c + 1) * 512]
                    E("tensor", w2,
                      lambda e, o=acc, l=lh, r=rh, st=(c == 0), sp=(c == 7):
                      e.matmul(o, l, r, start=st, stop=sp),
                      [(qa, 1)] if c == 7 else ())
                dst = ar[:, A_V4 + tt * 520:A_V4 + (tt + 1) * 520].rearrange(
                    "p (m d) -> p m d", d=65)[:, :, 0:64]
                src2 = acc.rearrange("p (m d) -> p m d", d=64)
                E("vector", [(qa, qa.n)],
                  lambda e, o=dst, i2=src2: e.tensor_copy(o, i2), [(vv, 1)])
                evac_hist.append(vv.n)
            vv_after_batch[vb] = vv.n

        def wload(hp):
            wb = A_WP[hp % 2]
            w = [(qa, qa_after_pair[hp - 2])] if hp >= 2 else []
            qin = d_wqkv[:, hp * 128:(hp + 1) * 128].rearrange(
                "(c p) m -> p c m", p=128)
            kin = d_wqkv[:, D + hp * 128:D + (hp + 1) * 128].rearrange(
                "(c p) m -> p c m", p=128)
            qo = ar[:, wb:wb + 2048].rearrange("p (c t m) -> p c t m",
                                               c=8, t=2)[:, :, 0, :]
            ko = ar[:, wb:wb + 2048].rearrange("p (c t m) -> p c t m",
                                               c=8, t=2)[:, :, 1, :]
            gdma(qo, qin, w)
            return gdma(ko, kin)

        CHUNKS = [(1, n) for n in range(8)] + [(0, n) for n in range(4)]

        def prologue_qkv(gw):
            hp, wb, qkb = 0, A_WP[0], A_QK[0]
            ev_hist = []
            fparts = []

            def emit_f2(j):
                kind, n = CHUNKS[j]
                fo = T[j % 2][:, 512:1024]
                E("tensor", [(ev, ev_hist[j])],
                  lambda e, o=fo, r=ar[:, A_TMP:A_TMP + 512]:
                  e.matmul(o, F2[:, :], r, start=True, stop=True), [(f2s, 1)])
                dcol = qkb + (2048 + n * 512 if kind == 1 else n * 512)
                E("vector", [(f2s, f2s.n)],
                  lambda e, o=ar[:, dcol:dcol + 512], i2=fo:
                  e.tensor_copy(o, i2), [(ev, 1)])

            evB_hist = []
            for i, (kind, n) in enumerate(CHUNKS):
                acc = T[i % 2][:, 0:512]
                for c in range(8):
                    w2 = []
                    if c == 0:
                        w2 = [(g, gw), (g, g_xt)]
                        if i >= 2:
                            w2.append((ev, ev_hist[i - 2]))
                        else:
                            w2.append((vv, vv_after_batch[0]))
                    lh = ar[:, wb + c * 256 + kind * 128:
                            wb + c * 256 + kind * 128 + 128]
                    rh = ar[:, A_XT + c * S + n * 512:
                            A_XT + c * S + n * 512 + 512]
                    E("tensor", w2,
                      lambda e, o=acc, l=lh, r=rh, st=(c == 0), sp=(c == 7):
                      e.matmul(o, l, r, start=st, stop=sp),
                      [(qa, 1)] if c == 7 else ())
                wA = [(qa, qa.n)]
                if i >= 1:
                    wA.append((f2s, f2s.n))
                E("vector", wA,
                  lambda e, o=ar[:, A_TMP:A_TMP + 512], i2=acc:
                  e.tensor_copy(o, i2), [(ev, 1)])
                ev_hist.append(ev.n)
                if i >= 1:
                    emit_f2(i - 1)
            emit_f2(len(CHUNKS) - 1)
            qa_after_pair[0] = qa.n
            ev_after_pair[0] = ev.n

        t3 = {"b0": None, "b1": None, "tmp": None}

        def make_micro_steps(hp, gw):
            wb = A_WP[hp % 2]
            qkb = A_QK[hp % 2]
            evA = {}
            steps = []
            for i, (kind, n) in enumerate(CHUNKS):
                for c in range(8):
                    def mm_step(i=i, kind=kind, n=n, c=c):
                        acc = T[3][:, 0:512]
                        w2 = []
                        if c == 0:
                            w2 = [(g, gw)]
                            if t3["b0"] is not None:
                                w2.append(t3["b0"])
                        lh = ar[:, wb + c * 256 + kind * 128:
                                wb + c * 256 + kind * 128 + 128]
                        rh = ar[:, A_XT + c * S + n * 512:
                                A_XT + c * S + n * 512 + 512]
                        E("tensor", w2,
                          lambda e, o=acc, l=lh, r=rh, st=(c == 0),
                          sp=(c == 7): e.matmul(o, l, r, start=st, stop=sp),
                          [(qa, 1)] if c == 7 else ())
                        if c == 7:
                            wA = [(qa, qa.n)]
                            if t3["tmp"] is not None:
                                wA.append(t3["tmp"])
                            E("vector", wA,
                              lambda e, o=ar[:, A_TMP:A_TMP + 512], i2=acc:
                              e.tensor_copy(o, i2), [(ev, 1)])
                            evA[i] = ev.n
                            t3["b0"] = (ev, ev.n)
                    steps.append(mm_step)

                def f2_step(i=i, kind=kind, n=n):
                    fo = T[3][:, 512:1024]
                    w2 = [(ev, evA[i])]
                    if t3["b1"] is not None:
                        w2.append(t3["b1"])
                    E("tensor", w2,
                      lambda e, o=fo, r=ar[:, A_TMP:A_TMP + 512]:
                      e.matmul(o, F2[:, :], r, start=True, stop=True),
                      [(f2s, 1)])
                    t3["tmp"] = (f2s, f2s.n)
                    dcol = qkb + (2048 + n * 512 if kind == 1 else n * 512)
                    E("vector", [(f2s, f2s.n)],
                      lambda e, o=ar[:, dcol:dcol + 512], i2=fo:
                      e.tensor_copy(o, i2), [(ev, 1)])
                    t3["b1"] = (ev, ev.n)

                steps.append(f2_step)
            return steps

        def emit_scores(hp, t):
            qc, kt = divmod(t, 32)
            Tg = hp * 128 + t
            tb = T[Tg % 2]
            qkb = A_QK[hp % 2]
            w2 = []
            if Tg >= 2:
                w2.append((ex, exp_hist[Tg - 2]))
            if t == 0:
                w2.append((ev, ev_after_pair[hp]))
                w2.append((vv, vv_after_batch[hp // 4]))
            kcol = qkb + 2048 + kt * 128
            qcol = qkb + qc * 512
            E("tensor", w2,
              lambda e, o=tb[:, 0:512], l=ar[0:64, kcol:kcol + 128],
              r=ar[0:64, qcol:qcol + 512]:
              e.matmul(o, l, r, start=True, stop=True,
                       tile_position=(0, 0)))
            E("tensor", [],
              lambda e, o=tb[:, 512:1024],
              l=ar[64:128, kcol:kcol + 128],
              r=ar[64:128, qcol:qcol + 512]:
              e.matmul(o, l, r, start=True, stop=True,
                       tile_position=(64, 0)), [(sc, 1)])
            eb = A_EXP[Tg % 2]
            we = [(sc, sc.n)]
            if Tg >= 2:
                we.append((pv, pv_hist[Tg - 2]))
            E("scalar", we,
              lambda e, o=ar[:, eb:eb + 1024], i2=tb[:, 0:1024]:
              e.activation(o, i2, AF.Exp, scale=np.float32(0.125)),
              [(ex, 1)])
            exp_hist[Tg] = ex.n

        def emit_pv(hp, t):
            qc, kt = divmod(t, 32)
            Tg = hp * 128 + t
            inst = hp * NQC + qc
            ob = T[2]
            eb = A_EXP[Tg % 2]
            sA = (hp % 4) * 130
            sB = sA + 65
            w3 = [(ex, exp_hist[Tg])]
            if kt == 0 and inst >= 1:
                w3.append((oe, inst))
            if t == 0:
                w3.append((vv, vv_after_batch[hp // 4]))
            E("tensor", w3,
              lambda e, o=ob[0:65, 0:512],
              l=ar[:, A_V4 + kt * 520 + sA:A_V4 + kt * 520 + sA + 65],
              r=ar[:, eb:eb + 512], st=(kt == 0), sp=(kt == NKT - 1):
              e.matmul(o, l, r, start=st, stop=sp))
            E("tensor", [],
              lambda e, o=ob[0:65, 512:1024],
              l=ar[:, A_V4 + kt * 520 + sB:A_V4 + kt * 520 + sB + 65],
              r=ar[:, eb + 512:eb + 1024], st=(kt == 0),
              sp=(kt == NKT - 1):
              e.matmul(o, l, r, start=st, stop=sp), [(pv, 1)])
            pv_hist[Tg] = pv.n

        def emit_stage_normalize(hp, qc):
            inst = hp * NQC + qc
            ob = T[2]
            w4 = [(pv, pv.n)]
            if inst >= 1:
                w4.append((nm, inst))
            E("vector", w4,
              lambda e, o=ar[0:65, A_STG:A_STG + 1024],
              i2=ob[0:65, 0:1024]: e.tensor_copy(o, i2), [(oe, 1)])
            oe_here = oe.n
            bcb = P_BC[inst % 2]

            def d_recipA(inst=inst, bcb=bcb, oe_here=oe_here):
                w5 = [(oe, oe_here)]
                if inst >= 2:
                    w5.append((y, scrw_y[inst - 2]))
                E("vector", w5,
                  lambda e, o=fp[64:65, bcb:bcb + 512],
                  i2=ar[64:65, A_STG:A_STG + 512]: e.reciprocal(o, i2))

            def d_recipB(inst=inst, bcb=bcb):
                E("vector", [],
                  lambda e, o=fp[64:65, bcb + 512:bcb + 1024],
                  i2=ar[64:65, A_STG + 512:A_STG + 1024]:
                  e.reciprocal(o, i2), [(rc, 1)])
                scr_row = d_scr[inst % 4:inst % 4 + 1, :]
                w6 = [(rc, rc.n)]
                if inst >= 4:
                    w6.append((y, bc_y[inst - 4]))
                scrw_y.append(ydma(scr_row, fp[64:65, bcb:bcb + 1024], w6))
                w7 = [(y, y.n)]
                if inst >= 2:
                    w7.append((nm, inst - 1))
                ydma(fp[0:64, bcb:bcb + 512],
                     scr_row[:, 0:512].to_broadcast((64, 512)), w7)
                bc_y.append(ydma(fp[0:64, bcb + 512:bcb + 1024],
                                 scr_row[:, 512:1024].to_broadcast((64, 512))))

            def d_mults(inst=inst, bcb=bcb, hp=hp, qc=qc):
                aot = A_AOT + hp * 2048 + qc * 512
                E("vector", [(y, bc_y[inst])],
                  lambda e, o=ar[0:64, aot:aot + 512],
                  i2=ar[0:64, A_STG:A_STG + 512],
                  b2=fp[0:64, bcb:bcb + 512]:
                  e.tensor_tensor(o, i2, b2, OP.mult))
                E("vector", [],
                  lambda e, o=ar[64:128, aot:aot + 512],
                  i2=ar[0:64, A_STG + 512:A_STG + 1024],
                  b2=fp[0:64, bcb + 512:bcb + 1024]:
                  e.tensor_tensor(o, i2, b2, OP.mult), [(nm, 1)])

            return [(1, d_recipA), (3, d_recipB), (7, d_mults)]

        gw0 = wload(0)
        vin0 = d_wqkv[:, 2 * D:2 * D + 512].rearrange("(c p) m -> p c m",
                                                      p=128)
        vout0 = ar[:, A_VW:A_VW + 4096].rearrange("p (c m) -> p c m", c=8)
        gv0 = gdma(vout0, vin0)
        for q in range(4):
            for c in range(8):
                gdma(ar[:, A_XT + c * S + q * 1024:
                        A_XT + c * S + (q + 1) * 1024],
                     d_xT[c * 128:(c + 1) * 128, q * 1024:(q + 1) * 1024])
            g_xt_q.append(g.n)
        g_xt = g.n
        v_batch(0)
        prologue_qkv(gw0)
        for hp in range(NPAIR):
            pend = []
            if hp < NPAIR - 1:
                gw = wload(hp + 1)
                pend = make_micro_steps(hp + 1, gw)
            emit_scores(hp, 0)
            dnorm = []
            for t in range(128):
                if t + 1 < 128:
                    emit_scores(hp, t + 1)
                if pend:
                    pend.pop(0)()
                while dnorm and dnorm[0][0] <= t:
                    dnorm.pop(0)[1]()
                emit_pv(hp, t)
                if t % 32 == 31:
                    for off, fn2 in emit_stage_normalize(hp, t // 32):
                        dnorm.append((t + off, fn2))
            while pend:
                pend.pop(0)()
            while dnorm:
                dnorm.pop(0)[1]()
            if hp < NPAIR - 1:
                qa_after_pair[hp + 1] = qa.n
                ev_after_pair[hp + 1] = ev.n
            if hp == 3:
                v_batch(1)

        win = d_wout.rearrange("(c p) m -> p c m", p=128)
        wo_out = ar[:, A_WO2:A_WO2 + 6144].rearrange("p (c m) -> p c m", c=6)
        gdma(wo_out, win[:, 0:6, :], [(sc, 896)])
        wo_out2 = ar[:, A_WO3:A_WO3 + 2048].rearrange("p (c m) -> p c m", c=2)
        gdma(wo_out2, win[:, 6:8, :], [(vv, vv_after_batch[1])])
        BOUT = BOUT2
        gwo = gdma(ar[0:1, BOUT:BOUT + 1024], d_bout)

        z1_hist = []
        for qt in range(NQT):
            par = qt % 2
            tp = T[par]
            w2 = [(z1, z1_hist[qt - 2])] if qt >= 2 else []
            xres_y = ydma(fp[:, P_XRES[par]:P_XRES[par] + 1024],
                          d_xres[qt * 128:(qt + 1) * 128, :], w2)
            for c in range(8):
                w3 = []
                if c == 0:
                    w3 = [(g, gwo), (nm, 29 + qt // 4)]
                    if qt >= 2:
                        w3.append((z1, z1_hist[qt - 2]))
                lh = ar[:, A_AOT + c * 2048 + qt * 128:
                        A_AOT + c * 2048 + qt * 128 + 128]
                wcol = A_WO2 + c * 1024 if c < 6 else A_WO3 + (c - 6) * 1024
                E("tensor", w3,
                  lambda e, o=tp[:, 0:512], l=lh,
                  r=ar[:, wcol:wcol + 512], st=(c == 0):
                  e.matmul(o, l, r, start=st, stop=False))
                E("tensor", [],
                  lambda e, o=tp[:, 512:1024], l=lh,
                  r=ar[:, wcol + 512:wcol + 1024], st=(c == 0):
                  e.matmul(o, l, r, start=st, stop=False))
            E("tensor", [],
              lambda e, o=tp[:, 0:512]:
              e.matmul(o, onesr[:, :], ar[0:1, BOUT:BOUT + 512], start=False,
                       stop=True))
            E("tensor", [],
              lambda e, o=tp[:, 512:1024]:
              e.matmul(o, onesr[:, :], ar[0:1, BOUT + 512:BOUT + 1024],
                       start=False, stop=True), [(pr, 1)])
            E("vector", [(pr, pr.n), (y, xres_y)],
              lambda e, o=ar[:, A_Z1 + qt * 1024:A_Z1 + (qt + 1) * 1024],
              i2=tp[:, 0:1024], x2=fp[:, P_XRES[par]:P_XRES[par] + 1024],
              a2=fp[:, P_ST + qt:P_ST + qt + 1]:
              e.scalar_tensor_tensor(o, i2, 1.0, x2, OP.mult, OP.add,
                                     accum_out=a2), [(z1, 1)])
            z1_hist.append(z1.n)
            E("scalar", [(z1, z1.n)],
              lambda e, o=ar[:, A_TRASH:A_TRASH + 1024],
              i2=ar[:, A_Z1 + qt * 1024:A_Z1 + (qt + 1) * 1024],
              a2=fp[:, P_ST + 16 + qt:P_ST + 16 + qt + 1]:
              e.activation(o, i2, AF.Square, accum_out=a2), [(sq, 1)])

        gwd = 0
        for part in range(4):
            wdin = d_wd.rearrange("(c p) m -> p c m",
                                  p=128)[:, part * 6:(part + 1) * 6, :]
            wdo = ar[:, A_WD:A_WD + 24576].rearrange(
                "p (c m) -> p c m", c=24)[:, part * 6:(part + 1) * 6, :]
            gwd = gdma(wdo, wdin, [(pr, 16)] if part == 0 else ())
        pre_gld = {}
        for hc in range(2):
            gin = d_wg[:, hc * 128:(hc + 1) * 128].rearrange(
                "(c p) m -> p c m", p=128)
            uin = d_wu[:, hc * 128:(hc + 1) * 128].rearrange(
                "(c p) m -> p c m", p=128)
            gdma(ar[:, A_WG[hc % 2]:A_WG[hc % 2] + 1024].rearrange(
                "p (c m) -> p c m", c=8), gin, [(pr, 16)])
            pre_gld[hc] = gdma(ar[:, A_WU[hc % 2]:A_WU[hc % 2] + 1024].
                               rearrange("p (c m) -> p c m", c=8), uin)

        def ln_stats(nt):
            def cS(k):
                return fp[:, P_ST + k * 16:P_ST + k * 16 + nt]
            chain("vector", [(z1, z1.n), (sq, sq.n), (z2, z2.n), (of, of.n)],
                  lambda e: e.tensor_scalar(cS(2), cS(0), 1.0 / D,
                                            None, OP.mult))
            chain("vector", [],
                  lambda e: e.tensor_tensor(cS(3), cS(2), cS(2), OP.mult))
            chain("vector", [],
                  lambda e: e.scalar_tensor_tensor(cS(4), cS(1), 1.0 / D,
                                                   cS(3), OP.mult,
                                                   OP.subtract))
            chain("scalar", [],
                  lambda e: e.activation(cS(5), cS(4), AF.Ln, bias=EPS))
            chain("scalar", [],
                  lambda e: e.activation(cS(6), cS(5), AF.Exp,
                                         scale=np.float32(-0.5)))
            chain("vector", [],
                  lambda e: e.scalar_tensor_tensor(cS(7), cS(2), -1.0,
                                                   cS(6), OP.mult, OP.mult))
            return ti.n

        st1 = ln_stats(16)
        te_hist = []
        for qt in range(NQT):
            E("vector", [(ti, st1)],
              lambda e, o=ar[:, A_XM + qt * 1024:A_XM + (qt + 1) * 1024],
              i2=ar[:, A_Z1 + qt * 1024:A_Z1 + (qt + 1) * 1024],
              s1=fp[:, P_ST + 96 + qt:P_ST + 96 + qt + 1],
              s2=fp[:, P_ST + 112 + qt:P_ST + 112 + qt + 1]:
              e.tensor_scalar(o, i2, s1, s2, OP.mult, OP.add), [(xm, 1)])
            tbit = T[2 + qt % 2][:, 0:512].bitcast(BF16)
            for j in range(8):
                w2 = []
                if j == 0:
                    w2 = [(xm, xm.n)]
                    if qt >= 2:
                        w2.append((te, te_hist[qt - 2]))
                    else:
                        w2.append((oe, 32))
                        w2.append((ev, ev.n))
                        w2.append((f2s, f2s.n))
                E("tensor", w2,
                  lambda e, o=tbit[:, j * 128:(j + 1) * 128],
                  i2=ar[:, A_XM + qt * 1024 + j * 128:
                         A_XM + qt * 1024 + j * 128 + 128]:
                  e.transpose(o, i2, id_b[:, :]),
                  [(tr, 1)] if j == 7 else ())
            xmt = ar[:, A_XMT:A_XMT + 16384].rearrange(
                "p (c t) -> p c t", c=8)[:, :, qt * 128:(qt + 1) * 128]
            E("vector", [(tr, tr.n)],
              lambda e, o=xmt,
              i2=tbit[:, 0:1024].rearrange("p (c t) -> p c t", c=8):
              e.tensor_copy(o, i2), [(te, 1)])
            te_hist.append(te.n)

        ha_hist = {}
        gu_hist = {}
        out_y_hist = []
        for th in range(2):
            for hc in range(NHC):
                wgb = A_WG[hc % 2]
                wub = A_WU[hc % 2]
                key_w = (th, hc - 2) if hc >= 2 else (th - 1, NHC - 2 + hc)
                if th == 0 and hc < 2:
                    gld = pre_gld[hc]
                else:
                    w2 = [(gu, gu_hist[key_w])]
                    gin = d_wg[:, hc * 128:(hc + 1) * 128].rearrange(
                        "(c p) m -> p c m", p=128)
                    uin = d_wu[:, hc * 128:(hc + 1) * 128].rearrange(
                        "(c p) m -> p c m", p=128)
                    gdma(ar[:, wgb:wgb + 1024].rearrange(
                        "p (c m) -> p c m", c=8), gin, w2)
                    gld = gdma(ar[:, wub:wub + 1024].rearrange(
                        "p (c m) -> p c m", c=8), uin)
                tg = T[2 * (hc % 2)]
                tu = T[2 * (hc % 2) + 1]
                for n in range(2):
                    for c in range(8):
                        w3 = []
                        if n == 0 and c == 0:
                            w3 = [(g, gld), (te, te.n)]
                            key_b = (th, hc - 2) if hc >= 2 else (
                                (th - 1, NHC - 2 + hc) if th > 0 else None)
                            if key_b is not None and key_b in ha_hist:
                                w3.append((ha, ha_hist[key_b]))
                            if th == 1 and hc < 2:
                                w3.append((z2, 8))
                        xcol = A_XMT + th * 1024 + n * 512
                        E("tensor", w3,
                          lambda e, o=tg[:, n * 512:(n + 1) * 512],
                          l=ar[:, wgb + c * 128:wgb + (c + 1) * 128],
                          r=ar[:, xcol + c * 2048:xcol + c * 2048 + 512],
                          st=(c == 0), sp=(c == 7):
                          e.matmul(o, l, r, start=st, stop=sp))
                        E("tensor", [],
                          lambda e, o=tu[:, n * 512:(n + 1) * 512],
                          l=ar[:, wub + c * 128:wub + (c + 1) * 128],
                          r=ar[:, xcol + c * 2048:xcol + c * 2048 + 512],
                          st=(c == 0), sp=(c == 7):
                          e.matmul(o, l, r, start=st, stop=sp),
                          [(gu, 1)] if (c == 7 and n == 1) else ())
                gu_hist[(th, hc)] = gu.n
                for n in range(2):
                    sb2 = A_SILU[n]
                    w3 = [(gu, gu.n)]
                    key_s = (th, hc - 1) if hc >= 1 else (th - 1, NHC - 1)
                    if key_s in ha_hist:
                        w3.append((ha, ha_hist[key_s]))
                    E("scalar", w3,
                      lambda e, o=ar[:, sb2:sb2 + 512],
                      i2=tg[:, n * 512:(n + 1) * 512]:
                      e.activation(o, i2, AF.Silu,
                                   scale=np.float32(1.0 / 127.0)), [(si, 1)])
                    hcol = A_HACT + hc * 1024 + n * 512
                    E("vector", [(si, si.n)],
                      lambda e, o=ar[:, hcol:hcol + 512],
                      i2=ar[:, sb2:sb2 + 512],
                      u2=tu[:, n * 512:(n + 1) * 512]:
                      e.tensor_tensor(o, i2, u2, OP.mult), [(ha, 1)])
                ha_hist[(th, hc)] = ha.n
            z2_hist = []
            for tt in range(8):
                par = tt % 2
                tp = T[par]
                for hc in range(NHC):
                    w3 = []
                    if hc == 0:
                        w3 = [(ha, ha_hist[(th, NHC - 1)]), (g, gwd)]
                        if tt >= 2:
                            w3.append((z2, z2_hist[tt - 2]))
                        else:
                            w3.append((gu, gu_hist[(th, NHC - 1)]))
                    lh = ar[:, A_HACT + hc * 1024 + tt * 128:
                            A_HACT + hc * 1024 + tt * 128 + 128]
                    E("tensor", w3,
                      lambda e, o=tp[:, 0:512], l=lh,
                      r=ar[:, A_WD + hc * 1024:A_WD + hc * 1024 + 512],
                      st=(hc == 0), sp=(hc == NHC - 1):
                      e.matmul(o, l, r, start=st, stop=sp))
                    E("tensor", [],
                      lambda e, o=tp[:, 512:1024], l=lh,
                      r=ar[:, A_WD + hc * 1024 + 512:A_WD + (hc + 1) * 1024],
                      st=(hc == 0), sp=(hc == NHC - 1):
                      e.matmul(o, l, r, start=st, stop=sp),
                      [(dn, 1)] if hc == NHC - 1 else ())
                qt = th * 8 + tt
                E("vector", [(dn, dn.n)],
                  lambda e, o=ar[:, A_Z2 + tt * 1024:A_Z2 + (tt + 1) * 1024],
                  i2=tp[:, 0:1024],
                  x2=ar[:, A_XM + qt * 1024:A_XM + (qt + 1) * 1024],
                  a2=fp[:, P_ST + tt:P_ST + tt + 1]:
                  e.scalar_tensor_tensor(o, i2, 1.0 / 16129.0, x2,
                                         OP.mult, OP.add, accum_out=a2),
                  [(z2, 1)])
                z2_hist.append(z2.n)
                E("scalar", [(z2, z2.n)],
                  lambda e, o=ar[:, A_TRASH:A_TRASH + 1024],
                  i2=ar[:, A_Z2 + tt * 1024:A_Z2 + (tt + 1) * 1024],
                  a2=fp[:, P_ST + 16 + tt:P_ST + 16 + tt + 1]:
                  e.activation(o, i2, AF.Square, accum_out=a2), [(sq, 1)])
            st2 = ln_stats(8)
            for tt in range(8):
                par = tt % 2
                qt = th * 8 + tt
                w3 = [(ti, st2)]
                if qt >= 2:
                    w3.append((y, out_y_hist[qt - 2]))
                E("vector", w3,
                  lambda e, o=fp[:, P_OUTF[par]:P_OUTF[par] + 1024],
                  i2=ar[:, A_Z2 + tt * 1024:A_Z2 + (tt + 1) * 1024],
                  s1=fp[:, P_ST + 96 + tt:P_ST + 96 + tt + 1],
                  s2=fp[:, P_ST + 112 + tt:P_ST + 112 + tt + 1]:
                  e.tensor_scalar(o, i2, s1, s2, OP.mult, OP.add), [(of, 1)])
                out_y_hist.append(
                    ydma(d_out[qt * 128:(qt + 1) * 128, :],
                         fp[:, P_OUTF[par]:P_OUTF[par] + 1024], [(of, of.n)]))

        E("sync", [(y, y.n)], lambda e: None)
        E("gpsimd", [(y, y.n)], lambda e: None)

        block = stk.enter_context(nc.Block())
        for name in ("tensor", "vector", "scalar", "sync", "gpsimd"):
            def runner(eng, name=name):
                for e2, fn in ops:
                    if e2 == name:
                        fn(eng)
            getattr(block, name)(runner)
    return nc


_NC_CACHE = None


def kernel(L0_attn_x, L0_attn_w_qkv, L0_attn_w_out, L0_attn_b_out,
           L0_attn_mask_logits, L0_ffn_w_gate, L0_ffn_w_up, L0_ffn_w_down):
    x = np.ascontiguousarray(np.asarray(L0_attn_x, dtype=np.float32))
    wqkv = np.ascontiguousarray(np.asarray(L0_attn_w_qkv, dtype=np.float32))
    wout = np.ascontiguousarray(np.asarray(L0_attn_w_out, dtype=np.float32))
    bout = np.ascontiguousarray(
        np.asarray(L0_attn_b_out, dtype=np.float32).reshape(1, D))
    mlog = np.ascontiguousarray(
        np.asarray(L0_attn_mask_logits, dtype=np.float32).reshape(33, 1))
    wgT = np.ascontiguousarray(np.asarray(L0_ffn_w_gate, dtype=np.int32).T)
    wuT = np.ascontiguousarray(np.asarray(L0_ffn_w_up, dtype=np.int32).T)
    wdT = np.ascontiguousarray(np.asarray(L0_ffn_w_down, dtype=np.int32).T)
    clt, rt = _f_consts()
    ident = np.eye(128, dtype=np.float32)

    global _NC_CACHE
    if _NC_CACHE is None:
        _NC_CACHE = build()
    nc = _NC_CACHE

    in_maps = []
    for c in range(8):
        b, hf = c // 2, c % 2
        xb = x[b]
        if hf == 0:
            xTc = np.ascontiguousarray(xb.T)
        else:
            xTc = np.ascontiguousarray(
                np.concatenate([xb[TOK:], xb[:TOK]], 0).T)
        in_maps.append({
            "xT": xTc,
            "xres": np.ascontiguousarray(xb[hf * TOK:(hf + 1) * TOK]),
            "wqkv": wqkv, "wout": wout, "bout": bout, "mlog": mlog,
            "clt": clt, "rt_": rt, "wgT": wgT, "wuT": wuT, "wdT": wdT,
            "ident": ident})
    res = run_bass_kernel_spmd(nc, in_maps, core_ids=list(range(8)))
    out = np.empty((B, S, D), np.float32)
    for c in range(8):
        b, hf = c // 2, c % 2
        out[b, hf * TOK:(hf + 1) * TOK] = np.asarray(res.results[c]["out"])
    return out


# version 18 (final)
# speedup vs baseline: 2.3624x; 1.0322x over previous
import numpy as np
import concourse.bass as bass
import concourse.mybir as mybir
from concourse.bass_utils import run_bass_kernel_spmd
from contextlib import ExitStack

F32 = mybir.dt.float32
BF16 = mybir.dt.bfloat16
I32 = mybir.dt.int32
AF = mybir.ActivationFunctionType
OP = mybir.AluOpType

B, S, D = 4, 4096, 1024
H, DH = 16, 64
HID = 3072
TOK = 2048
NPAIR = 8
NQC = TOK // 512
NKT = S // 128
NQT = TOK // 128
NHC = HID // 128
EPS = np.float32(1e-5)

A_XT = 0
A_WP = [32768, 34816]
A_QK = [36864, 43008]
A_WO2 = 36864
A_WO3 = 85248
BOUT2 = 87296
A_V4 = 49152
A_AOT = 65792
A_EXP = [82176, 83200]
A_TMP = 84224
A_VW = 85248
A_WO = 0
A_Z1 = 8192
A_XM = 24576
A_XMT = 40960
A_TRASH = 82176
A_HACT = 0
A_WD = 57344
A_WG = [83200, 84224]
A_WU = [85248, 86272]
A_SILU = [87296, 87808]
A_Z2 = 88320
ARENA = 96512

P_XRES = [0, 1024]
P_CLT = 0
P_BC = [2048, 3072]
P_OUTF = [2048, 3072]
P_ST = 4096
P_DEN = 4224
FPA = 4352


def _f_consts():
    n = DH
    f = np.arange(n // 2 + 1, dtype=np.float64)[:, None]
    a = np.arange(n, dtype=np.float64)[None, :]
    w = np.ones((n // 2 + 1, 1)); w[1:n // 2] = 2.0
    ang = 2 * np.pi * f * a / n
    clt = np.concatenate([w * np.cos(ang) / n, w * np.sin(ang) / n], 0)
    rt = np.concatenate([np.cos(ang), np.sin(ang)], 0)
    return clt.astype(np.float32), rt.astype(np.float32)


class Sem:
    def __init__(self, h):
        self.h = h
        self.n = 0


def build():
    nc = bass.Bass("TRN2", target_bir_lowering=False)
    d_xT = nc.dram_tensor("xT", [D, S], F32, kind="ExternalInput").ap()
    d_xres = nc.dram_tensor("xres", [TOK, D], F32, kind="ExternalInput").ap()
    d_wqkv = nc.dram_tensor("wqkv", [D, 3 * D], F32, kind="ExternalInput").ap()
    d_wout = nc.dram_tensor("wout", [D, D], F32, kind="ExternalInput").ap()
    d_bout = nc.dram_tensor("bout", [1, D], F32, kind="ExternalInput").ap()
    d_mlog = nc.dram_tensor("mlog", [33, 1], F32, kind="ExternalInput").ap()
    d_clt = nc.dram_tensor("clt", [66, 64], F32, kind="ExternalInput").ap()
    d_rt = nc.dram_tensor("rt_", [66, 64], F32, kind="ExternalInput").ap()
    d_wg = nc.dram_tensor("wgT", [D, HID], I32, kind="ExternalInput").ap()
    d_wu = nc.dram_tensor("wuT", [D, HID], I32, kind="ExternalInput").ap()
    d_wd = nc.dram_tensor("wdT", [HID, D], I32, kind="ExternalInput").ap()
    d_id = nc.dram_tensor("ident", [128, 128], F32, kind="ExternalInput").ap()
    d_out = nc.dram_tensor("out", [TOK, D], F32, kind="ExternalOutput").ap()
    d_scr = nc.dram_tensor("scr", [8, 1024], F32).ap()
    d_scrb = nc.dram_tensor("scrb", [4, 1024], BF16).ap()

    with ExitStack() as stk:
        def sbt(nm, shp, dt):
            return stk.enter_context(nc.sbuf_tensor(nm, shp, dt))

        ar = sbt("ar", [128, ARENA], BF16)
        fp = sbt("fp", [128, FPA], F32)
        id_b = sbt("id_b", [128, 128], BF16)
        F2 = sbt("F2", [128, 128], BF16)
        onesr = sbt("onesr", [1, 128], BF16)
        rt_b = sbt("rt_b", [66, 64], BF16)
        lhsF = sbt("lhsF", [66, 64], BF16)
        m2 = sbt("m2", [66, 1], F32)
        mrow = sbt("mrow", [66, 1], F32)
        db = sbt("db", [128, 8], BF16)
        T = [stk.enter_context(nc.psum_tensor(f"T{i}", [128, 1024], F32))
             for i in range(4)]

        names = ("g", "y", "sc", "pv", "qa", "f2", "pr", "tr", "gu", "dn",
                 "ev", "vv", "rc", "nm", "z1", "xm", "te", "ha", "z2", "of",
                 "ex", "sq", "si", "ti", "oe")
        SM = {n: Sem(stk.enter_context(nc.semaphore("s_" + n))) for n in names}
        (g, y, sc, pv, qa, f2s, pr, tr, gu, dn, ev, vv, rc, nm, z1, xm, te,
         ha, z2, of, ex, sq, si, ti, oe) = (SM[n] for n in names)

        ops = []

        def E(eng, waits, body, incs=()):
            waits = [(s.h, v) for (s, v) in waits if v > 0]
            incs2 = []
            for s, k in incs:
                s.n += k
                incs2.append((s.h, k))

            def fn(e, waits=waits, incs2=incs2, body=body):
                for h, v in waits:
                    e.wait_ge(h, v)
                ins = body(e)
                for h, k in incs2:
                    ins.then_inc(h, k)
            ops.append((eng, fn))

        def gdma(dst, src, waits=()):
            E("gpsimd", waits, lambda e, d=dst, s=src: e.dma_start(d, s),
              [(g, 16)])
            return g.n

        def ydma(dst, src, waits=()):
            E("sync", waits, lambda e, d=dst, s=src: e.dma_start(d, s),
              [(y, 16)])
            return y.n

        def chain(eng, waits, body):
            E(eng, list(waits) + [(ti, ti.n)], body, [(ti, 1)])

        ydma(mrow[0:33, :], d_mlog)
        ydma(mrow[33:66, :], d_mlog)
        ydma(fp[0:66, P_CLT:P_CLT + 64], d_clt)
        y_consts = y.n
        gdma(rt_b[:, :], d_rt)
        gdma(id_b[:, :], d_id)
        g_consts = g.n

        chain("scalar", [(y, y_consts)],
              lambda e: e.activation(m2[:, :], mrow[:, :], AF.Sigmoid))
        chain("vector", [],
              lambda e: e.tensor_scalar(lhsF[:, :], fp[0:66, P_CLT:P_CLT + 64],
                                        m2[:, 0:1], None, OP.mult))
        chain("tensor", [(g, g_consts)],
              lambda e: e.matmul(T[0][0:64, 0:64], lhsF[:, :], rt_b[:, :],
                                 start=True, stop=True))
        chain("vector", [], lambda e: e.memset(F2[:, :], 0.0))
        chain("vector", [],
              lambda e: e.tensor_copy(F2[0:64, 0:64], T[0][0:64, 0:64]))
        chain("vector", [],
              lambda e: e.tensor_copy(F2[64:128, 64:128], T[0][0:64, 0:64]))
        chain("vector", [], lambda e: e.memset(onesr[:, :], 1.0))
        f2_ready = ti.n

        A_STG = 89344
        qa_after_pair = {}
        ev_after_pair = {}
        vv_after_batch = {}
        g_xt_q = []
        exp_hist = {}
        scrw_y = []
        den_y = []
        wb_y = []
        rc_hist = []
        bc_y = []

        def v_batch(vb):
            if vb == 1:
                vin = d_wqkv[:, 2 * D + 512:2 * D + 1024].rearrange(
                    "(c p) m -> p c m", p=128)
                vout = ar[:, A_VW:A_VW + 4096].rearrange(
                    "p (c m) -> p c m", c=8)
                w = [(pv, 16)]
                gv = gdma(vout, vin, list(w))
            else:
                w = []
                gv = gv0
            ones_ap = ar[:, A_V4:A_V4 + 32 * 520].rearrange(
                "p (m d) -> p m d", d=65)[:, :, 64:65]
            E("vector", list(w),
              lambda e, o=ones_ap: e.memset(o, 1.0), [(vv, 1)])
            evac_hist = []
            for tt in range(32):
                acc = T[tt % 2][:, 0:512]
                for c in range(8):
                    w2 = []
                    if c == 0:
                        w2 = [(g, gv)]
                        if tt >= 2:
                            w2.append((vv, evac_hist[tt - 2]))
                        if vb == 1:
                            w2.append((pv, 16))
                        else:
                            w2.append((ti, f2_ready))
                            w2.append((g, g_xt_q[tt // 8]))
                    lh = ar[:, A_XT + c * S + tt * 128:
                            A_XT + c * S + tt * 128 + 128]
                    rh = ar[:, A_VW + c * 512:A_VW + (c + 1) * 512]
                    E("tensor", w2,
                      lambda e, o=acc, l=lh, r=rh, st=(c == 0), sp=(c == 7):
                      e.matmul(o, l, r, start=st, stop=sp),
                      [(qa, 1)] if c == 7 else ())
                dst = ar[:, A_V4 + tt * 520:A_V4 + (tt + 1) * 520].rearrange(
                    "p (m d) -> p m d", d=65)[:, :, 0:64]
                src2 = acc.rearrange("p (m d) -> p m d", d=64)
                E("vector", [(qa, qa.n)],
                  lambda e, o=dst, i2=src2: e.tensor_copy(o, i2), [(vv, 1)])
                evac_hist.append(vv.n)
            vv_after_batch[vb] = vv.n

        def wload(hp):
            wb = A_WP[hp % 2]
            w = [(qa, qa_after_pair[hp - 2])] if hp >= 2 else []
            qin = d_wqkv[:, hp * 128:(hp + 1) * 128].rearrange(
                "(c p) m -> p c m", p=128)
            kin = d_wqkv[:, D + hp * 128:D + (hp + 1) * 128].rearrange(
                "(c p) m -> p c m", p=128)
            qo = ar[:, wb:wb + 2048].rearrange("p (c t m) -> p c t m",
                                               c=8, t=2)[:, :, 0, :]
            ko = ar[:, wb:wb + 2048].rearrange("p (c t m) -> p c t m",
                                               c=8, t=2)[:, :, 1, :]
            gdma(qo, qin, w)
            return gdma(ko, kin)

        CHUNKS = [(1, n) for n in range(8)] + [(0, n) for n in range(4)]

        def prologue_qkv(gw):
            hp, wb, qkb = 0, A_WP[0], A_QK[0]
            ev_hist = []
            fparts = []

            def emit_f2(j):
                kind, n = CHUNKS[j]
                fo = T[j % 2][:, 512:1024]
                E("tensor", [(ev, ev_hist[j])],
                  lambda e, o=fo, r=ar[:, A_TMP:A_TMP + 512]:
                  e.matmul(o, F2[:, :], r, start=True, stop=True), [(f2s, 1)])
                dcol = qkb + (2048 + n * 512 if kind == 1 else n * 512)
                E("vector", [(f2s, f2s.n)],
                  lambda e, o=ar[:, dcol:dcol + 512], i2=fo:
                  e.tensor_copy(o, i2), [(ev, 1)])

            evB_hist = []
            for i, (kind, n) in enumerate(CHUNKS):
                acc = T[i % 2][:, 0:512]
                for c in range(8):
                    w2 = []
                    if c == 0:
                        w2 = [(g, gw), (g, g_xt)]
                        if i >= 2:
                            w2.append((ev, ev_hist[i - 2]))
                        else:
                            w2.append((vv, vv_after_batch[0]))
                    lh = ar[:, wb + c * 256 + kind * 128:
                            wb + c * 256 + kind * 128 + 128]
                    rh = ar[:, A_XT + c * S + n * 512:
                            A_XT + c * S + n * 512 + 512]
                    E("tensor", w2,
                      lambda e, o=acc, l=lh, r=rh, st=(c == 0), sp=(c == 7):
                      e.matmul(o, l, r, start=st, stop=sp),
                      [(qa, 1)] if c == 7 else ())
                wA = [(qa, qa.n)]
                if i >= 1:
                    wA.append((f2s, f2s.n))
                E("vector", wA,
                  lambda e, o=ar[:, A_TMP:A_TMP + 512], i2=acc:
                  e.tensor_copy(o, i2), [(ev, 1)])
                ev_hist.append(ev.n)
                if i >= 1:
                    emit_f2(i - 1)
            emit_f2(len(CHUNKS) - 1)
            qa_after_pair[0] = qa.n
            ev_after_pair[0] = ev.n

        t3 = {"b0": None, "b1": None, "tmp": None}

        def make_micro_steps(hp, gw):
            wb = A_WP[hp % 2]
            qkb = A_QK[hp % 2]
            evA = {}
            steps = []
            for i, (kind, n) in enumerate(CHUNKS):
                for c in range(8):
                    def mm_step(i=i, kind=kind, n=n, c=c):
                        acc = T[3][:, 0:512]
                        w2 = []
                        if c == 0:
                            w2 = [(g, gw)]
                            if t3["b0"] is not None:
                                w2.append(t3["b0"])
                        lh = ar[:, wb + c * 256 + kind * 128:
                                wb + c * 256 + kind * 128 + 128]
                        rh = ar[:, A_XT + c * S + n * 512:
                                A_XT + c * S + n * 512 + 512]
                        E("tensor", w2,
                          lambda e, o=acc, l=lh, r=rh, st=(c == 0),
                          sp=(c == 7): e.matmul(o, l, r, start=st, stop=sp),
                          [(qa, 1)] if c == 7 else ())
                        if c == 7:
                            wA = [(qa, qa.n)]
                            if t3["tmp"] is not None:
                                wA.append(t3["tmp"])
                            E("vector", wA,
                              lambda e, o=ar[:, A_TMP:A_TMP + 512], i2=acc:
                              e.tensor_copy(o, i2), [(ev, 1)])
                            evA[i] = ev.n
                            t3["b0"] = (ev, ev.n)
                    steps.append(mm_step)

                def f2_step(i=i, kind=kind, n=n):
                    fo = T[3][:, 512:1024]
                    w2 = [(ev, evA[i])]
                    if t3["b1"] is not None:
                        w2.append(t3["b1"])
                    E("tensor", w2,
                      lambda e, o=fo, r=ar[:, A_TMP:A_TMP + 512]:
                      e.matmul(o, F2[:, :], r, start=True, stop=True),
                      [(f2s, 1)])
                    t3["tmp"] = (f2s, f2s.n)
                    dcol = qkb + (2048 + n * 512 if kind == 1 else n * 512)
                    E("vector", [(f2s, f2s.n)],
                      lambda e, o=ar[:, dcol:dcol + 512], i2=fo:
                      e.tensor_copy(o, i2), [(ev, 1)])
                    t3["b1"] = (ev, ev.n)

                steps.append(f2_step)
            return steps

        def emit_scores(hp, t):
            qc, kt = divmod(t, 32)
            Tg = hp * 128 + t
            tb = T[Tg % 2]
            qkb = A_QK[hp % 2]
            w2 = []
            if t == 0:
                w2.append((ev, ev_after_pair[hp]))
                w2.append((vv, vv_after_batch[hp // 4]))
                if Tg >= 2:
                    w2.append((ex, exp_hist[Tg - 2]))
            kcol = qkb + 2048 + kt * 128
            qcol = qkb + qc * 512
            E("tensor", w2,
              lambda e, o=tb[:, 0:512], l=ar[0:64, kcol:kcol + 128],
              r=ar[0:64, qcol:qcol + 512]:
              e.matmul(o, l, r, start=True, stop=True,
                       tile_position=(0, 0)))
            E("tensor", [],
              lambda e, o=tb[:, 512:1024],
              l=ar[64:128, kcol:kcol + 128],
              r=ar[64:128, qcol:qcol + 512]:
              e.matmul(o, l, r, start=True, stop=True,
                       tile_position=(64, 0)), [(sc, 1)])
            eb = A_EXP[Tg % 2]
            we = [(sc, sc.n)]
            E("scalar", we,
              lambda e, o=ar[:, eb:eb + 1024], i2=tb[:, 0:1024]:
              e.activation(o, i2, AF.Exp, scale=np.float32(0.125)),
              [(ex, 1)])
            exp_hist[Tg] = ex.n

        def emit_pv(hp, t):
            qc, kt = divmod(t, 32)
            Tg = hp * 128 + t
            inst = hp * NQC + qc
            ob = T[2]
            eb = A_EXP[Tg % 2]
            sA = (hp % 4) * 130
            sB = sA + 65
            w3 = [(ex, exp_hist[Tg])]
            if kt == 0 and inst >= 1:
                w3.append((oe, inst))
            if t == 0:
                w3.append((vv, vv_after_batch[hp // 4]))
            E("tensor", w3,
              lambda e, o=ob[0:65, 0:512],
              l=ar[:, A_V4 + kt * 520 + sA:A_V4 + kt * 520 + sA + 65],
              r=ar[:, eb:eb + 512], st=(kt == 0), sp=(kt == NKT - 1):
              e.matmul(o, l, r, start=st, stop=sp))
            E("tensor", [],
              lambda e, o=ob[0:65, 512:1024],
              l=ar[:, A_V4 + kt * 520 + sB:A_V4 + kt * 520 + sB + 65],
              r=ar[:, eb + 512:eb + 1024], st=(kt == 0),
              sp=(kt == NKT - 1):
              e.matmul(o, l, r, start=st, stop=sp),
              [(pv, 1)] if kt == NKT - 1 else ())

        def emit_stage_normalize(hp, qc):
            inst = hp * NQC + qc
            ob = T[2]
            w4 = [(pv, pv.n)]
            if inst >= 1:
                w4.append((nm, inst))
            E("vector", w4,
              lambda e, o=ar[0:65, A_STG:A_STG + 1024],
              i2=ob[0:65, 0:1024]: e.tensor_copy(o, i2), [(oe, 1)])
            oe_here = oe.n
            bcb = P_BC[inst % 2]
            r1b = d_scrb[inst % 4:inst % 4 + 1, :]
            r2 = d_scr[inst % 4:inst % 4 + 1, :]
            w5 = [(oe, oe_here)]
            if inst >= 4:
                w5.append((y, den_y[inst - 4]))
            scrw_y.append(ydma(r1b, ar[64:65, A_STG:A_STG + 1024], w5))
            w6 = [(y, scrw_y[inst])]
            if inst >= 1:
                w6.append((rc, rc_hist[inst - 1]))
            den_y.append(ydma(db[:, :],
                              r1b.rearrange("o (p f) -> (o p) f", p=128), w6))

            def d_recip(inst=inst):
                E("vector", [(y, den_y[inst])],
                  lambda e, o=fp[:, P_DEN:P_DEN + 8], i2=db[:, :]:
                  e.tensor_copy(o, i2), [(rc, 1)])
                w7 = [(rc, rc.n)]
                if inst >= 1:
                    w7.append((y, wb_y[inst - 1]))
                E("vector", w7,
                  lambda e, o=fp[:, P_DEN + 8:P_DEN + 16],
                  i2=fp[:, P_DEN:P_DEN + 8]: e.reciprocal(o, i2), [(rc, 1)])
                rc_hist.append(rc.n)
                w8 = [(rc, rc.n)]
                if inst >= 4:
                    w8.append((y, bc_y[inst - 4]))
                wb_y.append(ydma(r2.rearrange("o (p f) -> (o p) f", p=128),
                                 fp[:, P_DEN + 8:P_DEN + 16], w8))
                w9 = [(y, wb_y[inst])]
                if inst >= 2:
                    w9.append((nm, inst - 1))
                ydma(fp[0:64, bcb:bcb + 512],
                     r2[:, 0:512].to_broadcast((64, 512)), w9)
                bc_y.append(ydma(fp[0:64, bcb + 512:bcb + 1024],
                                 r2[:, 512:1024].to_broadcast((64, 512))))

            def d_mults(inst=inst, bcb=bcb, hp=hp, qc=qc):
                aot = A_AOT + hp * 2048 + qc * 512
                E("vector", [(y, bc_y[inst])],
                  lambda e, o=ar[0:64, aot:aot + 512],
                  i2=ar[0:64, A_STG:A_STG + 512],
                  b2=fp[0:64, bcb:bcb + 512]:
                  e.tensor_tensor(o, i2, b2, OP.mult))
                E("vector", [],
                  lambda e, o=ar[64:128, aot:aot + 512],
                  i2=ar[0:64, A_STG + 512:A_STG + 1024],
                  b2=fp[0:64, bcb + 512:bcb + 1024]:
                  e.tensor_tensor(o, i2, b2, OP.mult), [(nm, 1)])

            return [(3, d_recip), (11, d_mults)]

        gw0 = wload(0)
        vin0 = d_wqkv[:, 2 * D:2 * D + 512].rearrange("(c p) m -> p c m",
                                                      p=128)
        vout0 = ar[:, A_VW:A_VW + 4096].rearrange("p (c m) -> p c m", c=8)
        gv0 = gdma(vout0, vin0)
        for q in range(4):
            for c in range(8):
                gdma(ar[:, A_XT + c * S + q * 1024:
                        A_XT + c * S + (q + 1) * 1024],
                     d_xT[c * 128:(c + 1) * 128, q * 1024:(q + 1) * 1024])
            g_xt_q.append(g.n)
        g_xt = g.n
        v_batch(0)
        prologue_qkv(gw0)
        for hp in range(NPAIR):
            pend = []
            if hp < NPAIR - 1:
                gw = wload(hp + 1)
                pend = make_micro_steps(hp + 1, gw)
            emit_scores(hp, 0)
            dnorm = []
            for t in range(128):
                if t + 1 < 128:
                    emit_scores(hp, t + 1)
                if pend:
                    pend.pop(0)()
                while dnorm and dnorm[0][0] <= t:
                    dnorm.pop(0)[1]()
                emit_pv(hp, t)
                if t % 32 == 31:
                    for off, fn2 in emit_stage_normalize(hp, t // 32):
                        dnorm.append((t + off, fn2))
            while pend:
                pend.pop(0)()
            while dnorm:
                dnorm.pop(0)[1]()
            if hp < NPAIR - 1:
                qa_after_pair[hp + 1] = qa.n
                ev_after_pair[hp + 1] = ev.n
            if hp == 3:
                v_batch(1)

        win = d_wout.rearrange("(c p) m -> p c m", p=128)
        wo_out = ar[:, A_WO2:A_WO2 + 6144].rearrange("p (c m) -> p c m", c=6)
        gdma(wo_out, win[:, 0:6, :], [(sc, 896)])
        wo_out2 = ar[:, A_WO3:A_WO3 + 2048].rearrange("p (c m) -> p c m", c=2)
        gdma(wo_out2, win[:, 6:8, :], [(vv, vv_after_batch[1])])
        BOUT = BOUT2
        gwo = gdma(ar[0:1, BOUT:BOUT + 1024], d_bout)

        z1_hist = []
        for qt in range(NQT):
            par = qt % 2
            tp = T[par]
            w2 = [(z1, z1_hist[qt - 2])] if qt >= 2 else []
            xres_y = ydma(fp[:, P_XRES[par]:P_XRES[par] + 1024],
                          d_xres[qt * 128:(qt + 1) * 128, :], w2)
            for c in range(8):
                w3 = []
                if c == 0:
                    w3 = [(g, gwo), (nm, 29 + qt // 4)]
                    if qt >= 2:
                        w3.append((z1, z1_hist[qt - 2]))
                lh = ar[:, A_AOT + c * 2048 + qt * 128:
                        A_AOT + c * 2048 + qt * 128 + 128]
                wcol = A_WO2 + c * 1024 if c < 6 else A_WO3 + (c - 6) * 1024
                E("tensor", w3,
                  lambda e, o=tp[:, 0:512], l=lh,
                  r=ar[:, wcol:wcol + 512], st=(c == 0):
                  e.matmul(o, l, r, start=st, stop=False))
                E("tensor", [],
                  lambda e, o=tp[:, 512:1024], l=lh,
                  r=ar[:, wcol + 512:wcol + 1024], st=(c == 0):
                  e.matmul(o, l, r, start=st, stop=False))
            E("tensor", [],
              lambda e, o=tp[:, 0:512]:
              e.matmul(o, onesr[:, :], ar[0:1, BOUT:BOUT + 512], start=False,
                       stop=True))
            E("tensor", [],
              lambda e, o=tp[:, 512:1024]:
              e.matmul(o, onesr[:, :], ar[0:1, BOUT + 512:BOUT + 1024],
                       start=False, stop=True), [(pr, 1)])
            E("vector", [(pr, pr.n), (y, xres_y)],
              lambda e, o=ar[:, A_Z1 + qt * 1024:A_Z1 + (qt + 1) * 1024],
              i2=tp[:, 0:1024], x2=fp[:, P_XRES[par]:P_XRES[par] + 1024],
              a2=fp[:, P_ST + qt:P_ST + qt + 1]:
              e.scalar_tensor_tensor(o, i2, 1.0, x2, OP.mult, OP.add,
                                     accum_out=a2), [(z1, 1)])
            z1_hist.append(z1.n)
            E("scalar", [(z1, z1.n)],
              lambda e, o=ar[:, A_TRASH:A_TRASH + 1024],
              i2=ar[:, A_Z1 + qt * 1024:A_Z1 + (qt + 1) * 1024],
              a2=fp[:, P_ST + 16 + qt:P_ST + 16 + qt + 1]:
              e.activation(o, i2, AF.Square, accum_out=a2), [(sq, 1)])

        pre_gld = {}
        for hc in range(2):
            gin = d_wg[:, hc * 128:(hc + 1) * 128].rearrange(
                "(c p) m -> p c m", p=128)
            uin = d_wu[:, hc * 128:(hc + 1) * 128].rearrange(
                "(c p) m -> p c m", p=128)
            gdma(ar[:, A_WG[hc % 2]:A_WG[hc % 2] + 1024].rearrange(
                "p (c m) -> p c m", c=8), gin, [(pr, 16)])
            pre_gld[hc] = gdma(ar[:, A_WU[hc % 2]:A_WU[hc % 2] + 1024].
                               rearrange("p (c m) -> p c m", c=8), uin)
        gwd = 0
        for part in range(4):
            wdin = d_wd.rearrange("(c p) m -> p c m",
                                  p=128)[:, part * 6:(part + 1) * 6, :]
            wdo = ar[:, A_WD:A_WD + 24576].rearrange(
                "p (c m) -> p c m", c=24)[:, part * 6:(part + 1) * 6, :]
            gwd = gdma(wdo, wdin, [(pr, 16)] if part == 0 else ())

        def ln_stats(nt):
            def cS(k):
                return fp[:, P_ST + k * 16:P_ST + k * 16 + nt]
            chain("vector", [(z1, z1.n), (sq, sq.n), (z2, z2.n), (of, of.n)],
                  lambda e: e.tensor_scalar(cS(2), cS(0), 1.0 / D,
                                            None, OP.mult))
            chain("vector", [],
                  lambda e: e.tensor_tensor(cS(3), cS(2), cS(2), OP.mult))
            chain("vector", [],
                  lambda e: e.scalar_tensor_tensor(cS(4), cS(1), 1.0 / D,
                                                   cS(3), OP.mult,
                                                   OP.subtract))
            chain("scalar", [],
                  lambda e: e.activation(cS(5), cS(4), AF.Ln, bias=EPS))
            chain("scalar", [],
                  lambda e: e.activation(cS(6), cS(5), AF.Exp,
                                         scale=np.float32(-0.5)))
            chain("vector", [],
                  lambda e: e.scalar_tensor_tensor(cS(7), cS(2), -1.0,
                                                   cS(6), OP.mult, OP.mult))
            return ti.n

        st1 = ln_stats(16)
        te_hist = []
        for qt in range(NQT):
            E("scalar", [(ti, st1)],
              lambda e, o=ar[:, A_XM + qt * 1024:A_XM + (qt + 1) * 1024],
              i2=ar[:, A_Z1 + qt * 1024:A_Z1 + (qt + 1) * 1024],
              s1=fp[:, P_ST + 96 + qt:P_ST + 96 + qt + 1],
              s2=fp[:, P_ST + 112 + qt:P_ST + 112 + qt + 1]:
              e.activation(o, i2, AF.Identity, scale=s1, bias=s2),
              [(xm, 1)])
            tbit = T[2 + qt % 2][:, 0:512].bitcast(BF16)
            for j in range(8):
                w2 = []
                if j == 0:
                    w2 = [(xm, xm.n)]
                    if qt >= 2:
                        w2.append((te, te_hist[qt - 2]))
                    else:
                        w2.append((oe, 32))
                        w2.append((ev, ev.n))
                        w2.append((f2s, f2s.n))
                E("tensor", w2,
                  lambda e, o=tbit[:, j * 128:(j + 1) * 128],
                  i2=ar[:, A_XM + qt * 1024 + j * 128:
                         A_XM + qt * 1024 + j * 128 + 128]:
                  e.transpose(o, i2, id_b[:, :]),
                  [(tr, 1)] if j == 7 else ())
            xmt = ar[:, A_XMT:A_XMT + 16384].rearrange(
                "p (c t) -> p c t", c=8)[:, :, qt * 128:(qt + 1) * 128]
            E("vector", [(tr, tr.n)],
              lambda e, o=xmt,
              i2=tbit[:, 0:1024].rearrange("p (c t) -> p c t", c=8):
              e.tensor_copy(o, i2), [(te, 1)])
            te_hist.append(te.n)

        ha_hist = {}
        gu_hist = {}
        out_y_hist = []
        for th in range(2):
            for hc in range(NHC):
                wgb = A_WG[hc % 2]
                wub = A_WU[hc % 2]
                key_w = (th, hc - 2) if hc >= 2 else (th - 1, NHC - 2 + hc)
                if th == 0 and hc < 2:
                    gld = pre_gld[hc]
                else:
                    w2 = [(gu, gu_hist[key_w])]
                    gin = d_wg[:, hc * 128:(hc + 1) * 128].rearrange(
                        "(c p) m -> p c m", p=128)
                    uin = d_wu[:, hc * 128:(hc + 1) * 128].rearrange(
                        "(c p) m -> p c m", p=128)
                    gdma(ar[:, wgb:wgb + 1024].rearrange(
                        "p (c m) -> p c m", c=8), gin, w2)
                    gld = gdma(ar[:, wub:wub + 1024].rearrange(
                        "p (c m) -> p c m", c=8), uin)
                tg = T[2 * (hc % 2)]
                tu = T[2 * (hc % 2) + 1]
                for n in range(2):
                    for c in range(8):
                        w3 = []
                        if n == 0 and c == 0:
                            w3 = [(g, gld), (te, te.n)]
                            key_b = (th, hc - 2) if hc >= 2 else (
                                (th - 1, NHC - 2 + hc) if th > 0 else None)
                            if key_b is not None and key_b in ha_hist:
                                w3.append((ha, ha_hist[key_b]))
                            if th == 1 and hc < 2:
                                w3.append((z2, 8))
                        xcol = A_XMT + th * 1024 + n * 512
                        E("tensor", w3,
                          lambda e, o=tg[:, n * 512:(n + 1) * 512],
                          l=ar[:, wgb + c * 128:wgb + (c + 1) * 128],
                          r=ar[:, xcol + c * 2048:xcol + c * 2048 + 512],
                          st=(c == 0), sp=(c == 7):
                          e.matmul(o, l, r, start=st, stop=sp))
                        E("tensor", [],
                          lambda e, o=tu[:, n * 512:(n + 1) * 512],
                          l=ar[:, wub + c * 128:wub + (c + 1) * 128],
                          r=ar[:, xcol + c * 2048:xcol + c * 2048 + 512],
                          st=(c == 0), sp=(c == 7):
                          e.matmul(o, l, r, start=st, stop=sp),
                          [(gu, 1)] if (c == 7 and n == 1) else ())
                gu_hist[(th, hc)] = gu.n
                for n in range(2):
                    sb2 = A_SILU[n]
                    w3 = [(gu, gu.n)]
                    key_s = (th, hc - 1) if hc >= 1 else (th - 1, NHC - 1)
                    if key_s in ha_hist:
                        w3.append((ha, ha_hist[key_s]))
                    E("scalar", w3,
                      lambda e, o=ar[:, sb2:sb2 + 512],
                      i2=tg[:, n * 512:(n + 1) * 512]:
                      e.activation(o, i2, AF.Silu,
                                   scale=np.float32(1.0 / 127.0)), [(si, 1)])
                    hcol = A_HACT + hc * 1024 + n * 512
                    E("vector", [(si, si.n)],
                      lambda e, o=ar[:, hcol:hcol + 512],
                      i2=ar[:, sb2:sb2 + 512],
                      u2=tu[:, n * 512:(n + 1) * 512]:
                      e.tensor_tensor(o, i2, u2, OP.mult), [(ha, 1)])
                ha_hist[(th, hc)] = ha.n
            z2_hist = []
            for tt in range(8):
                par = tt % 2
                tp = T[par]
                for hc in range(NHC):
                    w3 = []
                    if hc == 0:
                        w3 = [(ha, ha_hist[(th, NHC - 1)]), (g, gwd)]
                        if tt >= 2:
                            w3.append((z2, z2_hist[tt - 2]))
                        else:
                            w3.append((gu, gu_hist[(th, NHC - 1)]))
                    lh = ar[:, A_HACT + hc * 1024 + tt * 128:
                            A_HACT + hc * 1024 + tt * 128 + 128]
                    E("tensor", w3,
                      lambda e, o=tp[:, 0:512], l=lh,
                      r=ar[:, A_WD + hc * 1024:A_WD + hc * 1024 + 512],
                      st=(hc == 0), sp=(hc == NHC - 1):
                      e.matmul(o, l, r, start=st, stop=sp))
                    E("tensor", [],
                      lambda e, o=tp[:, 512:1024], l=lh,
                      r=ar[:, A_WD + hc * 1024 + 512:A_WD + (hc + 1) * 1024],
                      st=(hc == 0), sp=(hc == NHC - 1):
                      e.matmul(o, l, r, start=st, stop=sp),
                      [(dn, 1)] if hc == NHC - 1 else ())
                qt = th * 8 + tt
                E("vector", [(dn, dn.n)],
                  lambda e, o=ar[:, A_Z2 + tt * 1024:A_Z2 + (tt + 1) * 1024],
                  i2=tp[:, 0:1024],
                  x2=ar[:, A_XM + qt * 1024:A_XM + (qt + 1) * 1024],
                  a2=fp[:, P_ST + tt:P_ST + tt + 1]:
                  e.scalar_tensor_tensor(o, i2, 1.0 / 16129.0, x2,
                                         OP.mult, OP.add, accum_out=a2),
                  [(z2, 1)])
                z2_hist.append(z2.n)
                E("scalar", [(z2, z2.n)],
                  lambda e, o=ar[:, A_TRASH:A_TRASH + 1024],
                  i2=ar[:, A_Z2 + tt * 1024:A_Z2 + (tt + 1) * 1024],
                  a2=fp[:, P_ST + 16 + tt:P_ST + 16 + tt + 1]:
                  e.activation(o, i2, AF.Square, accum_out=a2), [(sq, 1)])
            st2 = ln_stats(8)
            for tt in range(8):
                par = tt % 2
                qt = th * 8 + tt
                w3 = [(ti, st2)]
                if qt >= 2:
                    w3.append((y, out_y_hist[qt - 2]))
                E("scalar", w3,
                  lambda e, o=fp[:, P_OUTF[par]:P_OUTF[par] + 1024],
                  i2=ar[:, A_Z2 + tt * 1024:A_Z2 + (tt + 1) * 1024],
                  s1=fp[:, P_ST + 96 + tt:P_ST + 96 + tt + 1],
                  s2=fp[:, P_ST + 112 + tt:P_ST + 112 + tt + 1]:
                  e.activation(o, i2, AF.Identity, scale=s1, bias=s2),
                  [(of, 1)])
                out_y_hist.append(
                    ydma(d_out[qt * 128:(qt + 1) * 128, :],
                         fp[:, P_OUTF[par]:P_OUTF[par] + 1024], [(of, of.n)]))

        E("sync", [(y, y.n)], lambda e: None)
        E("gpsimd", [(y, y.n)], lambda e: None)

        block = stk.enter_context(nc.Block())
        for name in ("tensor", "vector", "scalar", "sync", "gpsimd"):
            def runner(eng, name=name):
                for e2, fn in ops:
                    if e2 == name:
                        fn(eng)
            getattr(block, name)(runner)
    return nc


_NC_CACHE = None


def kernel(L0_attn_x, L0_attn_w_qkv, L0_attn_w_out, L0_attn_b_out,
           L0_attn_mask_logits, L0_ffn_w_gate, L0_ffn_w_up, L0_ffn_w_down):
    x = np.ascontiguousarray(np.asarray(L0_attn_x, dtype=np.float32))
    wqkv = np.ascontiguousarray(np.asarray(L0_attn_w_qkv, dtype=np.float32))
    wout = np.ascontiguousarray(np.asarray(L0_attn_w_out, dtype=np.float32))
    bout = np.ascontiguousarray(
        np.asarray(L0_attn_b_out, dtype=np.float32).reshape(1, D))
    mlog = np.ascontiguousarray(
        np.asarray(L0_attn_mask_logits, dtype=np.float32).reshape(33, 1))
    wgT = np.ascontiguousarray(np.asarray(L0_ffn_w_gate, dtype=np.int32).T)
    wuT = np.ascontiguousarray(np.asarray(L0_ffn_w_up, dtype=np.int32).T)
    wdT = np.ascontiguousarray(np.asarray(L0_ffn_w_down, dtype=np.int32).T)
    clt, rt = _f_consts()
    ident = np.eye(128, dtype=np.float32)

    global _NC_CACHE
    if _NC_CACHE is None:
        _NC_CACHE = build()
    nc = _NC_CACHE

    in_maps = []
    for c in range(8):
        b, hf = c // 2, c % 2
        xb = x[b]
        if hf == 0:
            xTc = np.ascontiguousarray(xb.T)
        else:
            xTc = np.ascontiguousarray(
                np.concatenate([xb[TOK:], xb[:TOK]], 0).T)
        in_maps.append({
            "xT": xTc,
            "xres": np.ascontiguousarray(xb[hf * TOK:(hf + 1) * TOK]),
            "wqkv": wqkv, "wout": wout, "bout": bout, "mlog": mlog,
            "clt": clt, "rt_": rt, "wgT": wgT, "wuT": wuT, "wdT": wdT,
            "ident": ident})
    res = run_bass_kernel_spmd(nc, in_maps, core_ids=list(range(8)))
    out = np.empty((B, S, D), np.float32)
    for c in range(8):
        b, hf = c // 2, c % 2
        out[b, hf * TOK:(hf + 1) * TOK] = np.asarray(res.results[c]["out"])
    return out


# version 19
# speedup vs baseline: 2.3700x; 1.0032x over previous
import numpy as np
import concourse.bass as bass
import concourse.mybir as mybir
from concourse.bass_utils import run_bass_kernel_spmd
from contextlib import ExitStack

F32 = mybir.dt.float32
BF16 = mybir.dt.bfloat16
I32 = mybir.dt.int32
AF = mybir.ActivationFunctionType
OP = mybir.AluOpType

B, S, D = 4, 4096, 1024
H, DH = 16, 64
HID = 3072
TOK = 2048
NPAIR = 8
NQC = TOK // 512
NKT = S // 128
NQT = TOK // 128
NHC = HID // 128
EPS = np.float32(1e-5)

A_XT = 0
A_WP = [32768, 34816]
A_QK = [36864, 43008]
A_WO2 = 36864
A_WO3 = 85248
BOUT2 = 87296
A_V4 = 49152
A_AOT = 65792
A_EXP = [82176, 83200]
A_TMP = 84224
A_VW = 85248
A_WO = 0
A_Z1 = 8192
A_XM = 24576
A_XMT = 40960
A_TRASH = 82176
A_HACT = 0
A_WD = 57344
A_WG = [83200, 84224]
A_WU = [85248, 86272]
A_SILU = [87296, 87808]
A_Z2 = 88320
ARENA = 96512

P_XRES = [0, 1024]
P_CLT = 0
P_BC = [2048, 3072]
P_OUTF = [2048, 3072]
P_ST = 4096
P_DEN = 4224
FPA = 4352


def _f_consts():
    n = DH
    f = np.arange(n // 2 + 1, dtype=np.float64)[:, None]
    a = np.arange(n, dtype=np.float64)[None, :]
    w = np.ones((n // 2 + 1, 1)); w[1:n // 2] = 2.0
    ang = 2 * np.pi * f * a / n
    clt = np.concatenate([w * np.cos(ang) / n, w * np.sin(ang) / n], 0)
    rt = np.concatenate([np.cos(ang), np.sin(ang)], 0)
    return clt.astype(np.float32), rt.astype(np.float32)


class Sem:
    def __init__(self, h):
        self.h = h
        self.n = 0


def build():
    nc = bass.Bass("TRN2", target_bir_lowering=False)
    d_xT = nc.dram_tensor("xT", [D, S], F32, kind="ExternalInput").ap()
    d_xres = nc.dram_tensor("xres", [TOK, D], F32, kind="ExternalInput").ap()
    d_wqkv = nc.dram_tensor("wqkv", [D, 3 * D], F32, kind="ExternalInput").ap()
    d_wout = nc.dram_tensor("wout", [D, D], F32, kind="ExternalInput").ap()
    d_bout = nc.dram_tensor("bout", [1, D], F32, kind="ExternalInput").ap()
    d_mlog = nc.dram_tensor("mlog", [33, 1], F32, kind="ExternalInput").ap()
    d_clt = nc.dram_tensor("clt", [66, 64], F32, kind="ExternalInput").ap()
    d_rt = nc.dram_tensor("rt_", [66, 64], F32, kind="ExternalInput").ap()
    d_wg = nc.dram_tensor("wgT", [D, HID], I32, kind="ExternalInput").ap()
    d_wu = nc.dram_tensor("wuT", [D, HID], I32, kind="ExternalInput").ap()
    d_wd = nc.dram_tensor("wdT", [HID, D], I32, kind="ExternalInput").ap()
    d_id = nc.dram_tensor("ident", [128, 128], F32, kind="ExternalInput").ap()
    d_out = nc.dram_tensor("out", [TOK, D], F32, kind="ExternalOutput").ap()
    d_scr = nc.dram_tensor("scr", [8, 1024], F32).ap()
    d_scrb = nc.dram_tensor("scrb", [4, 1024], BF16).ap()

    with ExitStack() as stk:
        def sbt(nm, shp, dt):
            return stk.enter_context(nc.sbuf_tensor(nm, shp, dt))

        ar = sbt("ar", [128, ARENA], BF16)
        fp = sbt("fp", [128, FPA], F32)
        id_b = sbt("id_b", [128, 128], BF16)
        F2 = sbt("F2", [128, 128], BF16)
        onesr = sbt("onesr", [1, 128], BF16)
        rt_b = sbt("rt_b", [66, 64], BF16)
        lhsF = sbt("lhsF", [66, 64], BF16)
        m2 = sbt("m2", [66, 1], F32)
        mrow = sbt("mrow", [66, 1], F32)
        db = sbt("db", [128, 8], BF16)
        T = [stk.enter_context(nc.psum_tensor(f"T{i}", [128, 1024], F32))
             for i in range(4)]

        names = ("g", "y", "sc", "pv", "qa", "f2", "pr", "tr", "gu", "dn",
                 "ev", "vv", "rc", "nm", "z1", "xm", "te", "ha", "z2", "of",
                 "ex", "sq", "si", "ti", "oe")
        SM = {n: Sem(stk.enter_context(nc.semaphore("s_" + n))) for n in names}
        (g, y, sc, pv, qa, f2s, pr, tr, gu, dn, ev, vv, rc, nm, z1, xm, te,
         ha, z2, of, ex, sq, si, ti, oe) = (SM[n] for n in names)

        ops = []

        def E(eng, waits, body, incs=()):
            waits = [(s.h, v) for (s, v) in waits if v > 0]
            incs2 = []
            for s, k in incs:
                s.n += k
                incs2.append((s.h, k))

            def fn(e, waits=waits, incs2=incs2, body=body):
                for h, v in waits:
                    e.wait_ge(h, v)
                ins = body(e)
                for h, k in incs2:
                    ins.then_inc(h, k)
            ops.append((eng, fn))

        def gdma(dst, src, waits=()):
            E("gpsimd", waits, lambda e, d=dst, s=src: e.dma_start(d, s),
              [(g, 16)])
            return g.n

        def ydma(dst, src, waits=()):
            E("sync", waits, lambda e, d=dst, s=src: e.dma_start(d, s),
              [(y, 16)])
            return y.n

        def chain(eng, waits, body):
            E(eng, list(waits) + [(ti, ti.n)], body, [(ti, 1)])

        ydma(mrow[0:33, :], d_mlog)
        ydma(mrow[33:66, :], d_mlog)
        ydma(fp[0:66, P_CLT:P_CLT + 64], d_clt)
        y_consts = y.n
        gdma(rt_b[:, :], d_rt)
        gdma(id_b[:, :], d_id)
        g_consts = g.n

        chain("scalar", [(y, y_consts)],
              lambda e: e.activation(m2[:, :], mrow[:, :], AF.Sigmoid))
        chain("vector", [],
              lambda e: e.tensor_scalar(lhsF[:, :], fp[0:66, P_CLT:P_CLT + 64],
                                        m2[:, 0:1], None, OP.mult))
        chain("tensor", [(g, g_consts)],
              lambda e: e.matmul(T[0][0:64, 0:64], lhsF[:, :], rt_b[:, :],
                                 start=True, stop=True))
        chain("vector", [], lambda e: e.memset(F2[:, :], 0.0))
        chain("vector", [],
              lambda e: e.tensor_copy(F2[0:64, 0:64], T[0][0:64, 0:64]))
        chain("vector", [],
              lambda e: e.tensor_copy(F2[64:128, 64:128], T[0][0:64, 0:64]))
        chain("vector", [], lambda e: e.memset(onesr[:, :], 1.0))
        f2_ready = ti.n

        A_STG = 89344
        qa_after_pair = {}
        ev_after_pair = {}
        vv_after_batch = {}
        g_xt_q = []
        exp_hist = {}
        scrw_y = []
        den_y = []
        wb_y = []
        rc_hist = []
        bc_y = []

        def v_batch(vb):
            if vb == 1:
                vin = d_wqkv[:, 2 * D + 512:2 * D + 1024].rearrange(
                    "(c p) m -> p c m", p=128)
                vout = ar[:, A_VW:A_VW + 4096].rearrange(
                    "p (c m) -> p c m", c=8)
                w = [(pv, 16)]
                gv = gdma(vout, vin, list(w))
            else:
                w = []
                gv = gv0
            ones_ap = ar[:, A_V4:A_V4 + 32 * 520].rearrange(
                "p (m d) -> p m d", d=65)[:, :, 64:65]
            E("vector", list(w),
              lambda e, o=ones_ap: e.memset(o, 1.0), [(vv, 1)])
            evac_hist = []
            for tt in range(32):
                acc = T[tt % 2][:, 0:512]
                for c in range(8):
                    w2 = []
                    if c == 0:
                        w2 = [(g, gv)]
                        if tt >= 2:
                            w2.append((vv, evac_hist[tt - 2]))
                        if vb == 1:
                            w2.append((pv, 16))
                        else:
                            w2.append((ti, f2_ready))
                            w2.append((g, g_xt_q[tt // 8]))
                    lh = ar[:, A_XT + c * S + tt * 128:
                            A_XT + c * S + tt * 128 + 128]
                    rh = ar[:, A_VW + c * 512:A_VW + (c + 1) * 512]
                    E("tensor", w2,
                      lambda e, o=acc, l=lh, r=rh, st=(c == 0), sp=(c == 7):
                      e.matmul(o, l, r, start=st, stop=sp),
                      [(qa, 1)] if c == 7 else ())
                dst = ar[:, A_V4 + tt * 520:A_V4 + (tt + 1) * 520].rearrange(
                    "p (m d) -> p m d", d=65)[:, :, 0:64]
                src2 = acc.rearrange("p (m d) -> p m d", d=64)
                E("vector", [(qa, qa.n)],
                  lambda e, o=dst, i2=src2: e.tensor_copy(o, i2), [(vv, 1)])
                evac_hist.append(vv.n)
            vv_after_batch[vb] = vv.n

        def wload(hp):
            wb = A_WP[hp % 2]
            w = [(qa, qa_after_pair[hp - 2])] if hp >= 2 else []
            qin = d_wqkv[:, hp * 128:(hp + 1) * 128].rearrange(
                "(c p) m -> p c m", p=128)
            kin = d_wqkv[:, D + hp * 128:D + (hp + 1) * 128].rearrange(
                "(c p) m -> p c m", p=128)
            qo = ar[:, wb:wb + 2048].rearrange("p (c t m) -> p c t m",
                                               c=8, t=2)[:, :, 0, :]
            ko = ar[:, wb:wb + 2048].rearrange("p (c t m) -> p c t m",
                                               c=8, t=2)[:, :, 1, :]
            gdma(qo, qin, w)
            return gdma(ko, kin)

        CHUNKS = [(1, n) for n in range(8)] + [(0, n) for n in range(4)]

        def prologue_qkv(gw):
            hp, wb, qkb = 0, A_WP[0], A_QK[0]
            ev_hist = []
            fparts = []

            def emit_f2(j):
                kind, n = CHUNKS[j]
                fo = T[j % 2][:, 512:1024]
                E("tensor", [(ev, ev_hist[j])],
                  lambda e, o=fo, r=ar[:, A_TMP:A_TMP + 512]:
                  e.matmul(o, F2[:, :], r, start=True, stop=True), [(f2s, 1)])
                dcol = qkb + (2048 + n * 512 if kind == 1 else n * 512)
                E("vector", [(f2s, f2s.n)],
                  lambda e, o=ar[:, dcol:dcol + 512], i2=fo:
                  e.tensor_copy(o, i2), [(ev, 1)])

            evB_hist = []
            for i, (kind, n) in enumerate(CHUNKS):
                acc = T[i % 2][:, 0:512]
                for c in range(8):
                    w2 = []
                    if c == 0:
                        w2 = [(g, gw), (g, g_xt_q[n // 2])]
                        if i >= 2:
                            w2.append((ev, ev_hist[i - 2]))
                        else:
                            w2.append((vv, vv_after_batch[0]))
                    lh = ar[:, wb + c * 256 + kind * 128:
                            wb + c * 256 + kind * 128 + 128]
                    rh = ar[:, A_XT + c * S + n * 512:
                            A_XT + c * S + n * 512 + 512]
                    E("tensor", w2,
                      lambda e, o=acc, l=lh, r=rh, st=(c == 0), sp=(c == 7):
                      e.matmul(o, l, r, start=st, stop=sp),
                      [(qa, 1)] if c == 7 else ())
                wA = [(qa, qa.n)]
                if i >= 1:
                    wA.append((f2s, f2s.n))
                E("vector", wA,
                  lambda e, o=ar[:, A_TMP:A_TMP + 512], i2=acc:
                  e.tensor_copy(o, i2), [(ev, 1)])
                ev_hist.append(ev.n)
                if i >= 1:
                    emit_f2(i - 1)
            emit_f2(len(CHUNKS) - 1)
            qa_after_pair[0] = qa.n
            ev_after_pair[0] = ev.n

        t3 = {"b0": None, "b1": None, "tmp": None}

        def make_micro_steps(hp, gw):
            wb = A_WP[hp % 2]
            qkb = A_QK[hp % 2]
            evA = {}
            steps = []
            for i, (kind, n) in enumerate(CHUNKS):
                for c in range(8):
                    def mm_step(i=i, kind=kind, n=n, c=c):
                        acc = T[3][:, 0:512]
                        w2 = []
                        if c == 0:
                            w2 = [(g, gw)]
                            if t3["b0"] is not None:
                                w2.append(t3["b0"])
                        lh = ar[:, wb + c * 256 + kind * 128:
                                wb + c * 256 + kind * 128 + 128]
                        rh = ar[:, A_XT + c * S + n * 512:
                                A_XT + c * S + n * 512 + 512]
                        E("tensor", w2,
                          lambda e, o=acc, l=lh, r=rh, st=(c == 0),
                          sp=(c == 7): e.matmul(o, l, r, start=st, stop=sp),
                          [(qa, 1)] if c == 7 else ())
                        if c == 7:
                            wA = [(qa, qa.n)]
                            if t3["tmp"] is not None:
                                wA.append(t3["tmp"])
                            E("vector", wA,
                              lambda e, o=ar[:, A_TMP:A_TMP + 512], i2=acc:
                              e.tensor_copy(o, i2), [(ev, 1)])
                            evA[i] = ev.n
                            t3["b0"] = (ev, ev.n)
                    steps.append(mm_step)

                def f2_step(i=i, kind=kind, n=n):
                    fo = T[3][:, 512:1024]
                    w2 = [(ev, evA[i])]
                    if t3["b1"] is not None:
                        w2.append(t3["b1"])
                    E("tensor", w2,
                      lambda e, o=fo, r=ar[:, A_TMP:A_TMP + 512]:
                      e.matmul(o, F2[:, :], r, start=True, stop=True),
                      [(f2s, 1)])
                    t3["tmp"] = (f2s, f2s.n)
                    dcol = qkb + (2048 + n * 512 if kind == 1 else n * 512)
                    E("vector", [(f2s, f2s.n)],
                      lambda e, o=ar[:, dcol:dcol + 512], i2=fo:
                      e.tensor_copy(o, i2), [(ev, 1)])
                    t3["b1"] = (ev, ev.n)

                steps.append(f2_step)
            return steps

        def emit_scores(hp, t):
            qc, kt = divmod(t, 32)
            Tg = hp * 128 + t
            tb = T[Tg % 2]
            qkb = A_QK[hp % 2]
            w2 = []
            if t == 0:
                w2.append((ev, ev_after_pair[hp]))
                w2.append((vv, vv_after_batch[hp // 4]))
                if Tg >= 2:
                    w2.append((ex, exp_hist[Tg - 2]))
            kcol = qkb + 2048 + kt * 128
            qcol = qkb + qc * 512
            E("tensor", w2,
              lambda e, o=tb[:, 0:512], l=ar[0:64, kcol:kcol + 128],
              r=ar[0:64, qcol:qcol + 512]:
              e.matmul(o, l, r, start=True, stop=True,
                       tile_position=(0, 0)))
            E("tensor", [],
              lambda e, o=tb[:, 512:1024],
              l=ar[64:128, kcol:kcol + 128],
              r=ar[64:128, qcol:qcol + 512]:
              e.matmul(o, l, r, start=True, stop=True,
                       tile_position=(64, 0)), [(sc, 1)])
            eb = A_EXP[Tg % 2]
            we = [(sc, sc.n)]
            E("scalar", we,
              lambda e, o=ar[:, eb:eb + 1024], i2=tb[:, 0:1024]:
              e.activation(o, i2, AF.Exp, scale=np.float32(0.125)),
              [(ex, 1)])
            exp_hist[Tg] = ex.n

        def emit_pv(hp, t):
            qc, kt = divmod(t, 32)
            Tg = hp * 128 + t
            inst = hp * NQC + qc
            ob = T[2]
            eb = A_EXP[Tg % 2]
            sA = (hp % 4) * 130
            sB = sA + 65
            w3 = [(ex, exp_hist[Tg])]
            if kt == 0 and inst >= 1:
                w3.append((oe, inst))
            if t == 0:
                w3.append((vv, vv_after_batch[hp // 4]))
            E("tensor", w3,
              lambda e, o=ob[0:65, 0:512],
              l=ar[:, A_V4 + kt * 520 + sA:A_V4 + kt * 520 + sA + 65],
              r=ar[:, eb:eb + 512], st=(kt == 0), sp=(kt == NKT - 1):
              e.matmul(o, l, r, start=st, stop=sp))
            E("tensor", [],
              lambda e, o=ob[0:65, 512:1024],
              l=ar[:, A_V4 + kt * 520 + sB:A_V4 + kt * 520 + sB + 65],
              r=ar[:, eb + 512:eb + 1024], st=(kt == 0),
              sp=(kt == NKT - 1):
              e.matmul(o, l, r, start=st, stop=sp),
              [(pv, 1)] if kt == NKT - 1 else ())

        def emit_stage_normalize(hp, qc):
            inst = hp * NQC + qc
            ob = T[2]
            w4 = [(pv, pv.n)]
            if inst >= 1:
                w4.append((nm, inst))
            E("vector", w4,
              lambda e, o=ar[0:65, A_STG:A_STG + 1024],
              i2=ob[0:65, 0:1024]: e.tensor_copy(o, i2), [(oe, 1)])
            oe_here = oe.n
            bcb = P_BC[inst % 2]
            r1b = d_scrb[inst % 4:inst % 4 + 1, :]
            r2 = d_scr[inst % 4:inst % 4 + 1, :]
            w5 = [(oe, oe_here)]
            if inst >= 4:
                w5.append((y, den_y[inst - 4]))
            scrw_y.append(ydma(r1b, ar[64:65, A_STG:A_STG + 1024], w5))
            w6 = [(y, scrw_y[inst])]
            if inst >= 1:
                w6.append((rc, rc_hist[inst - 1]))
            den_y.append(ydma(db[:, :],
                              r1b.rearrange("o (p f) -> (o p) f", p=128), w6))

            def d_recip(inst=inst):
                E("vector", [(y, den_y[inst])],
                  lambda e, o=fp[:, P_DEN:P_DEN + 8], i2=db[:, :]:
                  e.tensor_copy(o, i2), [(rc, 1)])
                w7 = [(rc, rc.n)]
                if inst >= 1:
                    w7.append((y, wb_y[inst - 1]))
                E("vector", w7,
                  lambda e, o=fp[:, P_DEN + 8:P_DEN + 16],
                  i2=fp[:, P_DEN:P_DEN + 8]: e.reciprocal(o, i2), [(rc, 1)])
                rc_hist.append(rc.n)
                w8 = [(rc, rc.n)]
                if inst >= 4:
                    w8.append((y, bc_y[inst - 4]))
                wb_y.append(ydma(r2.rearrange("o (p f) -> (o p) f", p=128),
                                 fp[:, P_DEN + 8:P_DEN + 16], w8))
                w9 = [(y, wb_y[inst])]
                if inst >= 2:
                    w9.append((nm, inst - 1))
                ydma(fp[0:64, bcb:bcb + 512],
                     r2[:, 0:512].to_broadcast((64, 512)), w9)
                bc_y.append(ydma(fp[0:64, bcb + 512:bcb + 1024],
                                 r2[:, 512:1024].to_broadcast((64, 512))))

            def d_mults(inst=inst, bcb=bcb, hp=hp, qc=qc):
                aot = A_AOT + hp * 2048 + qc * 512
                E("vector", [(y, bc_y[inst])],
                  lambda e, o=ar[0:64, aot:aot + 512],
                  i2=ar[0:64, A_STG:A_STG + 512],
                  b2=fp[0:64, bcb:bcb + 512]:
                  e.tensor_tensor(o, i2, b2, OP.mult))
                E("vector", [],
                  lambda e, o=ar[64:128, aot:aot + 512],
                  i2=ar[0:64, A_STG + 512:A_STG + 1024],
                  b2=fp[0:64, bcb + 512:bcb + 1024]:
                  e.tensor_tensor(o, i2, b2, OP.mult), [(nm, 1)])

            return [(3, d_recip), (11, d_mults)]

        gw0 = wload(0)
        vin0 = d_wqkv[:, 2 * D:2 * D + 512].rearrange("(c p) m -> p c m",
                                                      p=128)
        vout0 = ar[:, A_VW:A_VW + 4096].rearrange("p (c m) -> p c m", c=8)
        gv0 = gdma(vout0, vin0)
        for q in range(4):
            for c in range(8):
                gdma(ar[:, A_XT + c * S + q * 1024:
                        A_XT + c * S + (q + 1) * 1024],
                     d_xT[c * 128:(c + 1) * 128, q * 1024:(q + 1) * 1024])
            g_xt_q.append(g.n)
        g_xt = g.n
        v_batch(0)
        prologue_qkv(gw0)
        for hp in range(NPAIR):
            pend = []
            if hp < NPAIR - 1:
                gw = wload(hp + 1)
                pend = make_micro_steps(hp + 1, gw)
            emit_scores(hp, 0)
            dnorm = []
            for t in range(128):
                if t + 1 < 128:
                    emit_scores(hp, t + 1)
                if pend:
                    pend.pop(0)()
                while dnorm and dnorm[0][0] <= t:
                    dnorm.pop(0)[1]()
                emit_pv(hp, t)
                if t % 32 == 31:
                    for off, fn2 in emit_stage_normalize(hp, t // 32):
                        dnorm.append((t + off, fn2))
            while pend:
                pend.pop(0)()
            while dnorm:
                dnorm.pop(0)[1]()
            if hp < NPAIR - 1:
                qa_after_pair[hp + 1] = qa.n
                ev_after_pair[hp + 1] = ev.n
            if hp == 3:
                v_batch(1)

        win = d_wout.rearrange("(c p) m -> p c m", p=128)
        wo_out = ar[:, A_WO2:A_WO2 + 6144].rearrange("p (c m) -> p c m", c=6)
        gdma(wo_out, win[:, 0:6, :], [(sc, 896)])
        wo_out2 = ar[:, A_WO3:A_WO3 + 2048].rearrange("p (c m) -> p c m", c=2)
        gdma(wo_out2, win[:, 6:8, :], [(vv, vv_after_batch[1])])
        BOUT = BOUT2
        gwo = gdma(ar[0:1, BOUT:BOUT + 1024], d_bout)

        z1_hist = []
        for qt in range(NQT):
            par = qt % 2
            tp = T[par]
            w2 = [(z1, z1_hist[qt - 2])] if qt >= 2 else []
            xres_y = ydma(fp[:, P_XRES[par]:P_XRES[par] + 1024],
                          d_xres[qt * 128:(qt + 1) * 128, :], w2)
            for c in range(8):
                w3 = []
                if c == 0:
                    w3 = [(g, gwo), (nm, 29 + qt // 4)]
                    if qt >= 2:
                        w3.append((z1, z1_hist[qt - 2]))
                lh = ar[:, A_AOT + c * 2048 + qt * 128:
                        A_AOT + c * 2048 + qt * 128 + 128]
                wcol = A_WO2 + c * 1024 if c < 6 else A_WO3 + (c - 6) * 1024
                E("tensor", w3,
                  lambda e, o=tp[:, 0:512], l=lh,
                  r=ar[:, wcol:wcol + 512], st=(c == 0):
                  e.matmul(o, l, r, start=st, stop=False))
                E("tensor", [],
                  lambda e, o=tp[:, 512:1024], l=lh,
                  r=ar[:, wcol + 512:wcol + 1024], st=(c == 0):
                  e.matmul(o, l, r, start=st, stop=False))
            E("tensor", [],
              lambda e, o=tp[:, 0:512]:
              e.matmul(o, onesr[:, :], ar[0:1, BOUT:BOUT + 512], start=False,
                       stop=True))
            E("tensor", [],
              lambda e, o=tp[:, 512:1024]:
              e.matmul(o, onesr[:, :], ar[0:1, BOUT + 512:BOUT + 1024],
                       start=False, stop=True), [(pr, 1)])
            E("vector", [(pr, pr.n), (y, xres_y)],
              lambda e, o=ar[:, A_Z1 + qt * 1024:A_Z1 + (qt + 1) * 1024],
              i2=tp[:, 0:1024], x2=fp[:, P_XRES[par]:P_XRES[par] + 1024],
              a2=fp[:, P_ST + qt:P_ST + qt + 1]:
              e.scalar_tensor_tensor(o, i2, 1.0, x2, OP.mult, OP.add,
                                     accum_out=a2), [(z1, 1)])
            z1_hist.append(z1.n)
            E("scalar", [(z1, z1.n)],
              lambda e, o=ar[:, A_TRASH:A_TRASH + 1024],
              i2=ar[:, A_Z1 + qt * 1024:A_Z1 + (qt + 1) * 1024],
              a2=fp[:, P_ST + 16 + qt:P_ST + 16 + qt + 1]:
              e.activation(o, i2, AF.Square, accum_out=a2), [(sq, 1)])

        pre_gld = {}
        for hc in range(2):
            gin = d_wg[:, hc * 128:(hc + 1) * 128].rearrange(
                "(c p) m -> p c m", p=128)
            uin = d_wu[:, hc * 128:(hc + 1) * 128].rearrange(
                "(c p) m -> p c m", p=128)
            gdma(ar[:, A_WG[hc % 2]:A_WG[hc % 2] + 1024].rearrange(
                "p (c m) -> p c m", c=8), gin, [(pr, 16)])
            pre_gld[hc] = gdma(ar[:, A_WU[hc % 2]:A_WU[hc % 2] + 1024].
                               rearrange("p (c m) -> p c m", c=8), uin)
        gwd = 0
        for part in range(4):
            wdin = d_wd.rearrange("(c p) m -> p c m",
                                  p=128)[:, part * 6:(part + 1) * 6, :]
            wdo = ar[:, A_WD:A_WD + 24576].rearrange(
                "p (c m) -> p c m", c=24)[:, part * 6:(part + 1) * 6, :]
            gwd = gdma(wdo, wdin, [(pr, 16)] if part == 0 else ())

        def ln_stats(nt):
            def cS(k):
                return fp[:, P_ST + k * 16:P_ST + k * 16 + nt]
            chain("vector", [(z1, z1.n), (sq, sq.n), (z2, z2.n), (of, of.n)],
                  lambda e: e.tensor_scalar(cS(2), cS(0), 1.0 / D,
                                            None, OP.mult))
            chain("vector", [],
                  lambda e: e.tensor_tensor(cS(3), cS(2), cS(2), OP.mult))
            chain("vector", [],
                  lambda e: e.scalar_tensor_tensor(cS(4), cS(1), 1.0 / D,
                                                   cS(3), OP.mult,
                                                   OP.subtract))
            chain("scalar", [],
                  lambda e: e.activation(cS(5), cS(4), AF.Ln, bias=EPS))
            chain("scalar", [],
                  lambda e: e.activation(cS(6), cS(5), AF.Exp,
                                         scale=np.float32(-0.5)))
            chain("vector", [],
                  lambda e: e.scalar_tensor_tensor(cS(7), cS(2), -1.0,
                                                   cS(6), OP.mult, OP.mult))
            return ti.n

        st1 = ln_stats(16)
        te_hist = []
        for qt in range(NQT):
            E("scalar", [(ti, st1)],
              lambda e, o=ar[:, A_XM + qt * 1024:A_XM + (qt + 1) * 1024],
              i2=ar[:, A_Z1 + qt * 1024:A_Z1 + (qt + 1) * 1024],
              s1=fp[:, P_ST + 96 + qt:P_ST + 96 + qt + 1],
              s2=fp[:, P_ST + 112 + qt:P_ST + 112 + qt + 1]:
              e.activation(o, i2, AF.Identity, scale=s1, bias=s2),
              [(xm, 1)])
            tbit = T[2 + qt % 2][:, 0:512].bitcast(BF16)
            for j in range(8):
                w2 = []
                if j == 0:
                    w2 = [(xm, xm.n)]
                    if qt >= 2:
                        w2.append((te, te_hist[qt - 2]))
                    else:
                        w2.append((oe, 32))
                        w2.append((ev, ev.n))
                        w2.append((f2s, f2s.n))
                E("tensor", w2,
                  lambda e, o=tbit[:, j * 128:(j + 1) * 128],
                  i2=ar[:, A_XM + qt * 1024 + j * 128:
                         A_XM + qt * 1024 + j * 128 + 128]:
                  e.transpose(o, i2, id_b[:, :]),
                  [(tr, 1)] if j == 7 else ())
            xmt = ar[:, A_XMT:A_XMT + 16384].rearrange(
                "p (c t) -> p c t", c=8)[:, :, qt * 128:(qt + 1) * 128]
            E("vector", [(tr, tr.n)],
              lambda e, o=xmt,
              i2=tbit[:, 0:1024].rearrange("p (c t) -> p c t", c=8):
              e.tensor_copy(o, i2), [(te, 1)])
            te_hist.append(te.n)

        ha_hist = {}
        gu_hist = {}
        out_y_hist = []
        for th in range(2):
            for hc in range(NHC):
                wgb = A_WG[hc % 2]
                wub = A_WU[hc % 2]
                key_w = (th, hc - 2) if hc >= 2 else (th - 1, NHC - 2 + hc)
                if th == 0 and hc < 2:
                    gld = pre_gld[hc]
                else:
                    w2 = [(gu, gu_hist[key_w])]
                    gin = d_wg[:, hc * 128:(hc + 1) * 128].rearrange(
                        "(c p) m -> p c m", p=128)
                    uin = d_wu[:, hc * 128:(hc + 1) * 128].rearrange(
                        "(c p) m -> p c m", p=128)
                    gdma(ar[:, wgb:wgb + 1024].rearrange(
                        "p (c m) -> p c m", c=8), gin, w2)
                    gld = gdma(ar[:, wub:wub + 1024].rearrange(
                        "p (c m) -> p c m", c=8), uin)
                tg = T[2 * (hc % 2)]
                tu = T[2 * (hc % 2) + 1]
                for n in range(2):
                    for c in range(8):
                        w3 = []
                        if n == 0 and c == 0:
                            w3 = [(g, gld), (te, te.n)]
                            key_b = (th, hc - 2) if hc >= 2 else (
                                (th - 1, NHC - 2 + hc) if th > 0 else None)
                            if key_b is not None and key_b in ha_hist:
                                w3.append((ha, ha_hist[key_b]))
                            if th == 1 and hc < 2:
                                w3.append((z2, 8))
                        xcol = A_XMT + th * 1024 + n * 512
                        E("tensor", w3,
                          lambda e, o=tg[:, n * 512:(n + 1) * 512],
                          l=ar[:, wgb + c * 128:wgb + (c + 1) * 128],
                          r=ar[:, xcol + c * 2048:xcol + c * 2048 + 512],
                          st=(c == 0), sp=(c == 7):
                          e.matmul(o, l, r, start=st, stop=sp))
                        E("tensor", [],
                          lambda e, o=tu[:, n * 512:(n + 1) * 512],
                          l=ar[:, wub + c * 128:wub + (c + 1) * 128],
                          r=ar[:, xcol + c * 2048:xcol + c * 2048 + 512],
                          st=(c == 0), sp=(c == 7):
                          e.matmul(o, l, r, start=st, stop=sp),
                          [(gu, 1)] if (c == 7 and n == 1) else ())
                gu_hist[(th, hc)] = gu.n
                for n in range(2):
                    sb2 = A_SILU[n]
                    w3 = [(gu, gu.n)]
                    key_s = (th, hc - 1) if hc >= 1 else (th - 1, NHC - 1)
                    if key_s in ha_hist:
                        w3.append((ha, ha_hist[key_s]))
                    E("scalar", w3,
                      lambda e, o=ar[:, sb2:sb2 + 512],
                      i2=tg[:, n * 512:(n + 1) * 512]:
                      e.activation(o, i2, AF.Silu,
                                   scale=np.float32(1.0 / 127.0)), [(si, 1)])
                    hcol = A_HACT + hc * 1024 + n * 512
                    E("vector", [(si, si.n)],
                      lambda e, o=ar[:, hcol:hcol + 512],
                      i2=ar[:, sb2:sb2 + 512],
                      u2=tu[:, n * 512:(n + 1) * 512]:
                      e.tensor_tensor(o, i2, u2, OP.mult), [(ha, 1)])
                ha_hist[(th, hc)] = ha.n
            z2_hist = []
            for tt in range(8):
                par = tt % 2
                tp = T[par]
                for hc in range(NHC):
                    w3 = []
                    if hc == 0:
                        w3 = [(ha, ha_hist[(th, NHC - 1)]), (g, gwd)]
                        if tt >= 2:
                            w3.append((z2, z2_hist[tt - 2]))
                        else:
                            w3.append((gu, gu_hist[(th, NHC - 1)]))
                    lh = ar[:, A_HACT + hc * 1024 + tt * 128:
                            A_HACT + hc * 1024 + tt * 128 + 128]
                    E("tensor", w3,
                      lambda e, o=tp[:, 0:512], l=lh,
                      r=ar[:, A_WD + hc * 1024:A_WD + hc * 1024 + 512],
                      st=(hc == 0), sp=(hc == NHC - 1):
                      e.matmul(o, l, r, start=st, stop=sp))
                    E("tensor", [],
                      lambda e, o=tp[:, 512:1024], l=lh,
                      r=ar[:, A_WD + hc * 1024 + 512:A_WD + (hc + 1) * 1024],
                      st=(hc == 0), sp=(hc == NHC - 1):
                      e.matmul(o, l, r, start=st, stop=sp),
                      [(dn, 1)] if hc == NHC - 1 else ())
                qt = th * 8 + tt
                E("vector", [(dn, dn.n)],
                  lambda e, o=ar[:, A_Z2 + tt * 1024:A_Z2 + (tt + 1) * 1024],
                  i2=tp[:, 0:1024],
                  x2=ar[:, A_XM + qt * 1024:A_XM + (qt + 1) * 1024],
                  a2=fp[:, P_ST + tt:P_ST + tt + 1]:
                  e.scalar_tensor_tensor(o, i2, 1.0 / 16129.0, x2,
                                         OP.mult, OP.add, accum_out=a2),
                  [(z2, 1)])
                z2_hist.append(z2.n)
                E("scalar", [(z2, z2.n)],
                  lambda e, o=ar[:, A_TRASH:A_TRASH + 1024],
                  i2=ar[:, A_Z2 + tt * 1024:A_Z2 + (tt + 1) * 1024],
                  a2=fp[:, P_ST + 16 + tt:P_ST + 16 + tt + 1]:
                  e.activation(o, i2, AF.Square, accum_out=a2), [(sq, 1)])
            st2 = ln_stats(8)
            for tt in range(8):
                par = tt % 2
                qt = th * 8 + tt
                w3 = [(ti, st2)]
                if qt >= 2:
                    w3.append((y, out_y_hist[qt - 2]))
                E("scalar", w3,
                  lambda e, o=fp[:, P_OUTF[par]:P_OUTF[par] + 1024],
                  i2=ar[:, A_Z2 + tt * 1024:A_Z2 + (tt + 1) * 1024],
                  s1=fp[:, P_ST + 96 + tt:P_ST + 96 + tt + 1],
                  s2=fp[:, P_ST + 112 + tt:P_ST + 112 + tt + 1]:
                  e.activation(o, i2, AF.Identity, scale=s1, bias=s2),
                  [(of, 1)])
                out_y_hist.append(
                    ydma(d_out[qt * 128:(qt + 1) * 128, :],
                         fp[:, P_OUTF[par]:P_OUTF[par] + 1024], [(of, of.n)]))

        E("sync", [(y, y.n)], lambda e: None)
        E("gpsimd", [(y, y.n)], lambda e: None)

        block = stk.enter_context(nc.Block())
        for name in ("tensor", "vector", "scalar", "sync", "gpsimd"):
            def runner(eng, name=name):
                for e2, fn in ops:
                    if e2 == name:
                        fn(eng)
            getattr(block, name)(runner)
    return nc


_NC_CACHE = None


def kernel(L0_attn_x, L0_attn_w_qkv, L0_attn_w_out, L0_attn_b_out,
           L0_attn_mask_logits, L0_ffn_w_gate, L0_ffn_w_up, L0_ffn_w_down):
    x = np.ascontiguousarray(np.asarray(L0_attn_x, dtype=np.float32))
    wqkv = np.ascontiguousarray(np.asarray(L0_attn_w_qkv, dtype=np.float32))
    wout = np.ascontiguousarray(np.asarray(L0_attn_w_out, dtype=np.float32))
    bout = np.ascontiguousarray(
        np.asarray(L0_attn_b_out, dtype=np.float32).reshape(1, D))
    mlog = np.ascontiguousarray(
        np.asarray(L0_attn_mask_logits, dtype=np.float32).reshape(33, 1))
    wgT = np.ascontiguousarray(np.asarray(L0_ffn_w_gate, dtype=np.int32).T)
    wuT = np.ascontiguousarray(np.asarray(L0_ffn_w_up, dtype=np.int32).T)
    wdT = np.ascontiguousarray(np.asarray(L0_ffn_w_down, dtype=np.int32).T)
    clt, rt = _f_consts()
    ident = np.eye(128, dtype=np.float32)

    global _NC_CACHE
    if _NC_CACHE is None:
        _NC_CACHE = build()
    nc = _NC_CACHE

    in_maps = []
    for c in range(8):
        b, hf = c // 2, c % 2
        xb = x[b]
        if hf == 0:
            xTc = np.ascontiguousarray(xb.T)
        else:
            xTc = np.ascontiguousarray(
                np.concatenate([xb[TOK:], xb[:TOK]], 0).T)
        in_maps.append({
            "xT": xTc,
            "xres": np.ascontiguousarray(xb[hf * TOK:(hf + 1) * TOK]),
            "wqkv": wqkv, "wout": wout, "bout": bout, "mlog": mlog,
            "clt": clt, "rt_": rt, "wgT": wgT, "wuT": wuT, "wdT": wdT,
            "ident": ident})
    res = run_bass_kernel_spmd(nc, in_maps, core_ids=list(range(8)))
    out = np.empty((B, S, D), np.float32)
    for c in range(8):
        b, hf = c // 2, c % 2
        out[b, hf * TOK:(hf + 1) * TOK] = np.asarray(res.results[c]["out"])
    return out
